# version 09
# baseline (speedup 1.0000x reference)
import math
import bisect
import numpy as np
import ml_dtypes
import concourse.bass as bass
import concourse.mybir as mybir
from concourse.bass_utils import run_bass_kernel_spmd

F32 = mybir.dt.float32
BF16 = mybir.dt.bfloat16
U8 = mybir.dt.uint8
AF = mybir.ActivationFunctionType
ALU = mybir.AluOpType
AXX = mybir.AxisListType.X
BF = ml_dtypes.bfloat16

S = 2048
D = 2048
H = 16
HD = 128
R = 512
RD = 64
HID = 5632
HT = HID // 128
NB = 2
LNEPS = 1e-5
QEPS = 1e-8
SCL = 1.0 / math.sqrt(HD)
MAGIC = 12582912.0

ENGS = ("pe", "act", "dve", "pool", "sp")
QUEUES = ("q_sp", "q_act", "q_pool")
Q_OF = {"q_sp": "sp", "q_act": "act", "q_pool": "pool"}
SAME_ENGINE_SYNC = True


class Buf:
    __slots__ = ("name", "w", "r", "grp")

    def __init__(self, name):
        self.name = name
        self.w = None
        self.r = []
        self.grp = False


class Em:

    def __init__(self):
        self.ops = {e: [] for e in ENGS}
        self.clock_n = {c: 0 for c in ENGS + QUEUES}
        self.need = {c: set() for c in ENGS}
        self.bufs = {}
        self.dbg_deps = []

    def B(self, name):
        b = self.bufs.get(name)
        if b is None:
            b = self.bufs[name] = Buf(name)
        return b

    def op(self, eng, fn, reads=(), writes=(), dma=False, grp=None):
        clock = ("q_" + eng) if dma else eng
        onum = self.clock_n[clock]
        deps = []
        for b in reads:
            assert not b.grp, f"read of open accum group {b.name}"
            if b.w is not None:
                deps.append(b.w)
        first = grp in (None, "start")
        for b in writes:
            if first:
                assert not b.grp, f"write-start on open group {b.name}"
                if b.w is not None:
                    deps.append(b.w)
                deps.extend(b.r)
            else:
                assert b.grp, f"mid/stop on closed group {b.name}"
        waits = []
        for d in deps:
            c, i = d
            if c == clock and not dma:
                if eng == "pe" or not SAME_ENGINE_SYNC:
                    continue
            if d not in waits:
                waits.append(d)
                if c in ENGS:
                    self.need[c].add(i)
        rec = (clock, onum)
        for b in reads:
            b.r.append(rec)
        for b in writes:
            if grp in (None, "stop"):
                b.w = rec
                b.r = []
                b.grp = False
            else:
                b.grp = True
        self.ops[eng].append((fn, waits, clock, onum, dma, None))
        self.dbg_deps.append((clock, onum, tuple(deps)))
        self.clock_n[clock] += 1

    def fence(self):
        snap = dict(self.clock_n)
        for e in ENGS:
            waits = []
            for c in ENGS:
                if c == e:
                    continue
                if snap[c] > 0:
                    waits.append((c, snap[c] - 1))
                    self.need[c].add(snap[c] - 1)
            for c in QUEUES:
                n = snap[c]
                for r in range(min(n, self.M)):
                    i = (n - 1 - r) // self.M * self.M + r
                    if i < n:
                        waits.append((c, i))
            self.ops[e].append((None, waits, e, None, False, None))

    def wait_all(self, eng, clocks):
        self.ops[eng].append((None, None, eng, None, False, tuple(clocks)))

    M = 16

    def finalize(self, nc):
        self.sems = {}
        self._cms = []
        for c in ENGS:
            cm = nc.semaphore("sem_" + c)
            self.sems[c] = cm.__enter__()
            self._cms.append(cm)
        self.qsems = {}
        for c in QUEUES:
            self.qsems[c] = []
            for r in range(self.M):
                cm = nc.semaphore(f"sem_{c}_{r}")
                self.qsems[c].append(cm.__enter__())
                self._cms.append(cm)
        self.incmap = {c: sorted(self.need[c]) for c in ENGS}

    def q_target(self, c, i):
        return (c, i % self.M), 16 * (i // self.M + 1)

    def queue_final_targets(self, c, n=None):
        n = self.clock_n[c] if n is None else n
        out = []
        for r in range(self.M):
            cnt = (n - r + self.M - 1) // self.M if n > r else 0
            if cnt > 0:
                out.append(((c, r), 16 * cnt))
        return out

    def _sem_of(self, key):
        if isinstance(key, tuple):
            return self.qsems[key[0]][key[1]]
        return self.sems[key]

    def emit_engine(self, ename, eng):
        observed = {}
        markset = self.need.get(ename, set())

        def wt(key, v):
            if v > observed.get(key, 0):
                eng.wait_ge(self._sem_of(key), v)
                observed[key] = v

        def resolve(c, i):
            if c in ENGS:
                return (c,), bisect.bisect_right(self.incmap[c], i)
            key, v = self.q_target(c, i)
            return key, v

        for (fn, waits, clock, onum, dma, wall) in self.ops[ename]:
            if wall is not None:
                for c in wall:
                    if c in ENGS:
                        wt(c, len(self.incmap[c]))
                    else:
                        for key, v in self.queue_final_targets(c):
                            wt(key, v)
                continue
            if waits:
                for (c, i) in waits:
                    if c in ENGS:
                        wt(c, bisect.bisect_right(self.incmap[c], i))
                    else:
                        key, v = self.q_target(c, i)
                        wt(key, v)
            if fn is None:
                continue
            if dma:
                if onum >= self.M:
                    key, v = self.q_target(clock, onum - self.M)
                    wt(key, v)
                ins = fn(eng)
                ins.then_inc(self.qsems[clock][onum % self.M], 16)
            else:
                ins = fn(eng)
                if onum in markset:
                    ins.then_inc(self.sems[clock], 1)

    def stats(self):
        return {e: len(self.ops[e]) for e in ENGS}


def build_nc(stop_after="full", dbg=()):
    nc = bass.Bass("TRN2", target_bir_lowering=False)
    em = Em()
    B = em.B

    def din(name, shape, dtype=F32):
        return nc.dram_tensor(name, shape, dtype, kind="ExternalInput").ap()

    def dout(name, shape, dtype=F32):
        return nc.dram_tensor(name, shape, dtype, kind="ExternalOutput").ap()

    def dscr(name, shape, dtype=BF16):
        return nc.dram_tensor(name, shape, dtype, kind="Internal").ap()

    xT = din("xT", [NB, 16, 128, S])
    xr = din("xr", [NB, S, D])
    wq_c = din("wq_c", [H, 128, 16, 128])
    wkvd_c = din("wkvd_c", [128, 16, 4, 128])
    wkup_c = din("wkup_c", [H, 128, 4, 64])
    wkrope_c = din("wkrope_c", [128, 16, 64])
    wvup_c = din("wvup_c", [128, 4, 2048])
    wo_c = din("wo_c", [128, 16, 4, 512])
    w1_c = din("w1_c", [HT, 128, 16, 128])
    w2_c = din("w2_c", [HT, 128, 16, 128])
    w3_c = din("w3_c", [HT, 128, 4, 512])
    c_ones = din("c_ones", [128, 128])
    c_sperm = din("c_sperm", [64, 64])
    c_cos = din("c_cos", [64, S])
    c_sin = din("c_sin", [64, S])
    c_mask = din("c_mask", [128, 128])
    c_scal = din("c_scal", [128, 3])

    y = dout("y", [NB, S, D], F32)

    xnsp = dscr("xnsp", [NB, 16, 128, S])
    osp = dscr("osp", [NB, 16, 128, 16, 128])
    w3q = dscr("w3q", [HT, 128, 4, 512])
    gsc = dscr("gsc", [3, 128, 1], F32)
    gsc2 = dscr("gsc2", [3, 1, 1], F32)

    dbg_out = {}
    if "xn" in dbg:
        dbg_out["dbg_xn"] = dout("dbg_xn", [NB, 16, 128, S], BF16)
    if "ckv" in dbg:
        dbg_out["dbg_ckv"] = dout("dbg_ckv", [4, 128, S], BF16)
    if "o" in dbg:
        dbg_out["dbg_o"] = dout("dbg_o", [NB, 16, 128, 16, 128], BF16)
    if "hid" in dbg:
        dbg_out["dbg_hid"] = dout("dbg_hid", [HT, 128, 1024], BF16)
    if "wq" in dbg:
        dbg_out["dbg_w1q"] = dout("dbg_w1q", [HT, 128, 16, 128], BF16)
        dbg_out["dbg_w2q"] = dout("dbg_w2q", [HT, 128, 16, 128], BF16)
        dbg_out["dbg_w3q"] = dout("dbg_w3q", [HT, 128, 4, 512], BF16)
        dbg_out["dbg_gam"] = dout("dbg_gam", [128, 8], F32)
    if "qk" in dbg:
        dbg_out["dbg_qstk"] = dout("dbg_qstk", [128, S], BF16)
        dbg_out["dbg_kstk"] = dout("dbg_kstk", [128, S], BF16)
        dbg_out["dbg_rr"] = dout("dbg_rr", [4, 128, 512], F32)
        dbg_out["dbg_krope"] = dout("dbg_krope", [128, S], BF16)
        dbg_out["dbg_kraw"] = dout("dbg_kraw", [128, S], BF16)

    ARENA = 212352
    arena = nc.alloc_sbuf_tensor("arena", [128, ARENA], U8).ap()
    _off = [0]

    def reg(nbytes, align=512):
        o = (_off[0] + align - 1) // align * align
        _off[0] = o + nbytes
        assert _off[0] <= ARENA, f"sbuf overflow {_off[0]}"
        return o

    def view(off, nelem, dtype):
        isz = 2 if dtype == BF16 else 4
        assert off % isz == 0
        return arena[:, off:off + nelem * isz].bitcast(dtype)

    o = reg(256); ones_bf = view(o, 128, BF16)
    o = reg(512); ones_f = view(o, 128, F32)
    o = reg(128); sperm = view(o, 64, BF16)
    o = reg(4096); cosb = view(o, 2048, BF16)
    o = reg(4096); sinb = view(o, 2048, BF16)
    o = reg(512); maskb = view(o, 128, F32)
    o = reg(512); qsum = view(o, 128, F32)
    o = reg(512); small = view(o, 128, F32)
    o = reg(512); cscal = view(o, 128, F32)
    o = reg(512); grow = view(o, 128, F32)
    o_quant = reg(8448)
    NQIN = 3
    qin = [view(o_quant + i * 2048, 512, F32) for i in range(NQIN)]
    qout = [view(o_quant + 6144, 512, BF16), view(o_quant + 7168, 512, BF16)]
    o_xn = reg(65536); xnv = view(o_xn, 16 * S, BF16)
    o_vall = reg(65536); vall = view(o_vall, 16 * 2048, BF16)
    o_ckv = reg(16384); ckv = view(o_ckv, 4 * S, BF16)
    o_krope = reg(4096); krope = view(o_krope, S, BF16)
    o_head = reg(36352)
    wqs_v = [view(o_head, 2048, BF16), view(o_head + 4096, 2048, BF16)]
    qstk = view(o_head + 8192, S, BF16)
    kstk = view(o_head + 12288, S, BF16)
    wkups = view(o_head + 16384, 512, BF16)
    qraw = view(o_head + 17408, S, BF16)
    rtmp = view(o_head + 21504, 1024, F32)
    rtmp2 = view(o_head + 25600, 1024, F32)
    pts = view(o_head + 29696, 3 * 512, BF16)
    rln = view(o_head + 32768, 512, F32)
    rrec = view(o_head + 34816, 2 * 512, F32)
    assert o_head + 34816 <= o_head + 36352

    HB = o_head
    wqs_v = [view(HB + 0, 2048, BF16), view(HB + 4096, 2048, BF16)]
    qstk = view(HB + 8192, S, BF16)
    kstk = view(HB + 12288, S, BF16)
    qraw = view(HB + 16384, S, BF16)
    rtmp = view(HB + 20480, 1024, F32)
    rtmp2 = view(HB + 24576, 1024, F32)
    pts = view(HB + 28672, 3 * 512, BF16)
    rln = view(HB + 31744, 512, F32)
    rrec = view(HB + 33792, 512, F32)
    wkups = view(HB + 35840, 256, BF16)

    obf = view(o_quant, 2 * 512, BF16)
    o_obf = reg(2048)
    obf = view(o_obf, 2 * 512, BF16)

    ln_mean = view(HB + 0, 2048, F32)
    ln_rinv = view(HB + 8192, 2048, F32)
    ln_sq = [view(HB + 16384, 2048, BF16), view(HB + 20480, 2048, BF16)]
    ln_tmp = view(HB + 24576, 2048, F32)
    prow = view(HB + 0, 8192, BF16)

    def hid(t):
        if t < 32:
            return view(o_xn + t * 2048, 1024, BF16)
        if t < 40:
            return view(o_ckv + (t - 32) * 2048, 1024, BF16)
        if t < 42:
            return view(o_krope + (t - 40) * 2048, 1024, BF16)
        return view(HB + (t - 42) * 2048, 1024, BF16)
    xns = view(o_vall, 16 * 1024, BF16)
    w1f = [view(o_vall + 32768, 2048, F32), view(o_vall + 40960, 2048, F32)]
    w2f = [view(o_vall + 49152, 2048, F32), view(o_vall + 57344, 2048, F32)]
    w3col = view(o_vall, 44 * 512, BF16)
    wocol = view(o_vall + 45056, 16 * 512, BF16)
    ocol = [view(o_vall + 61440, 2048, BF16), view(HB + 4096, 2048, BF16)]
    w1s = [view(HB + 8704, 2048, BF16), view(HB + 12800, 2048, BF16)]
    w2s = [view(HB + 16896, 2048, BF16), view(HB + 20992, 2048, BF16)]
    sgb = view(HB + 25088, 1024, BF16)
    xres = [view(HB + 27136, 512, F32), view(HB + 29184, 512, F32)]
    outb = [view(HB + 31232, 512, F32), view(HB + 33280, 512, F32)]

    psA = nc.alloc_psum_tensor("psA", [128, 2048], F32).ap()
    psB = nc.alloc_psum_tensor("psB", [128, 2048], F32).ap()

    def ps(i):
        t = psA if i < 4 else psB
        return t[:, (i % 4) * 512:((i % 4) + 1) * 512]
    pbank = [B(f"psum{i}") for i in range(8)]

    def dma(queue, out, in_, reads=(), writes=()):
        eng = Q_OF[queue]
        em.op(eng, lambda e, o_=out, i_=in_: e.dma_start(out=o_, in_=i_),
              reads=list(reads), writes=list(writes), dma=True)

    def mm(out, lhsT, rhs, start, stop, reads=(), wbank=(), grp=None):
        em.op("pe", lambda e, o_=out, l_=lhsT, r_=rhs, s1=start, s2=stop:
              e.matmul(o_, l_, r_, start=s1, stop=s2),
              reads=list(reads), writes=list(wbank), grp=grp)

    def grp_of(i, n):
        if n == 1:
            return None
        return "start" if i == 0 else ("stop" if i == n - 1 else "mid")

    dma("q_pool", ones_bf[:], c_ones, writes=[B("ones_bf")])
    dma("q_sp", ones_f[:], c_ones, writes=[B("ones_f")])
    dma("q_pool", sperm[64:128, :], c_sperm, writes=[B("sperm")])
    dma("q_pool", cosb[64:128, :], c_cos, writes=[B("cosb")])
    dma("q_pool", sinb[64:128, :], c_sin, writes=[B("sinb")])
    dma("q_sp", maskb[:], c_mask, writes=[B("maskb")])
    dma("q_sp", cscal[:, 0:3], c_scal, writes=[B("cscal")])
    nc.const_aps.register(F32, LNEPS, cscal[:, 0:1])
    nc.const_aps.register(F32, MAGIC, cscal[:, 1:2])
    nc.const_aps.register(F32, -MAGIC, cscal[:, 2:3])

    QW = [(w1_c, None), (w2_c, None), (w3_c, w3q)]

    def gen_quant():
        for widx, (wsrc, wdst) in enumerate(QW):
            src = wsrc.rearrange("t p a c -> t p (a c)")
            dst = wdst.rearrange("t p a c -> t p (a c)") if wdst is not None else None
            nchunk = HT * 4
            FOLD = 112
            qsB = B("qsum")
            qfB = B("qfold")
            acc_col = small[:, widx:widx + 1]
            accB = B(f"qaccf{widx}")
            nfold = (nchunk + FOLD - 1) // FOLD
            for f in range(nfold):
                lo, hi = f * FOLD, min((f + 1) * FOLD, nchunk)
                for k in range(lo, hi):
                    t, s4 = divmod(k, 4)
                    sl = k % NQIN
                    tile = B(f"qin{sl}")
                    dma("q_sp", qin[sl][:], src[t][:, s4 * 512:(s4 + 1) * 512],
                        writes=[tile])
                    em.op("act", lambda e, o_=qout[k % 2][:], i_=qin[sl][:],
                          a_=qsum[:, k - lo:k - lo + 1]:
                          e.activation(o_, i_, AF.Abs, accum_out=a_),
                          reads=[tile], writes=[qsB, B(f"qout{k % 2}")],
                          grp=None)
                    if (k - lo) % 2 == 1:
                        yield
                em.op("dve", lambda e, o_=qsum[:, 120 + f:121 + f],
                      i_=qsum[:, 0:hi - lo]:
                      e.reduce_sum(o_, i_, axis=AXX),
                      reads=[qsB], writes=[qfB], grp=grp_of(f, nfold))
                yield
            em.op("dve", lambda e, o_=acc_col, i_=qsum[:, 120:120 + nfold]:
                  e.reduce_sum(o_, i_, axis=AXX), reads=[qfB], writes=[accB])
            gscB = B(f"gsc{widx}")
            dma("q_sp", gsc[widx], acc_col, reads=[accB], writes=[gscB])
            dma("q_sp", grow[0:1, 0:128],
                gsc[widx].rearrange("p x -> x (p)"),
                reads=[gscB], writes=[B("grow")])
            nelem = float(HID * D)
            em.op("dve", lambda e, o_=grow[0:1, 120:121], i_=grow[0:1, 0:128]:
                  e.reduce_sum(o_, i_, axis=AXX),
                  reads=[B("grow")], writes=[B("grow")])
            em.op("dve", lambda e, o_=grow[0:1, 121:122], i_=grow[0:1, 120:121]:
                  e.tensor_scalar(o_, i_, 1.0 / nelem, QEPS, ALU.mult, ALU.add),
                  reads=[B("grow")], writes=[B("grow")])
            em.op("dve", lambda e, o_=grow[0:1, 122:123], i_=grow[0:1, 121:122]:
                  e.reciprocal(o_, i_),
                  reads=[B("grow")], writes=[B("grow")])
            gs2B = B(f"gsc2_{widx}")
            dma("q_sp", gsc2[widx], grow[0:1, 122:123],
                reads=[B("grow")], writes=[gs2B])
            rcol = small[:, 4 + widx:5 + widx]
            dma("q_sp", rcol, gsc2[widx].broadcast_to([128, 1]),
                reads=[gs2B], writes=[B(f"qrec{widx}")])
            yield
            continue
            rB = B(f"qrec{widx}")
            for k in range(nchunk):
                t, s4 = divmod(k, 4)
                sl = k % NQIN
                tile = B(f"qin{sl}")
                osl = k % 2
                obB = B(f"qout{osl}")
                dma("q_sp", qin[sl][:], src[t][:, s4 * 512:(s4 + 1) * 512],
                    writes=[tile])
                em.op("dve", lambda e, o_=qin[sl][:], s_=rcol:
                      e.tensor_scalar(o_, o_, s_, MAGIC, ALU.mult, ALU.add),
                      reads=[tile, rB], writes=[tile])
                em.op("dve", lambda e, o_=qin[sl][:]:
                      e.tensor_scalar(o_, o_, MAGIC, -1.0, ALU.subtract, ALU.max),
                      reads=[tile], writes=[tile])
                em.op("dve", lambda e, o_=qout[osl][:], i_=qin[sl][:]:
                      e.tensor_scalar_min(o_, i_, 1.0),
                      reads=[tile], writes=[obB])
                dma("q_sp", dst[t][:, s4 * 512:(s4 + 1) * 512], qout[osl][:],
                    reads=[obB], writes=[B(f"w{widx}q_{t}")])
                if k % 8 == 7:
                    yield

    def gen_apply():
        widx = 2
        wsrc, wdst = QW[widx]
        src_ = wsrc.rearrange("t p a c -> t p (a c)")
        dst_ = wdst.rearrange("t p a c -> t p (a c)")
        rcol = small[:, 4 + widx:5 + widx]
        rB = B(f"qrec{widx}")
        nchunk = HT * 4
        for k in range(nchunk):
            t, s4 = divmod(k, 4)
            sl = k % NQIN
            tile = B(f"qin{sl}")
            osl = k % 2
            obB = B(f"qout{osl}")
            dma("q_sp", qin[sl][:], src_[t][:, s4 * 512:(s4 + 1) * 512],
                writes=[tile])
            em.op("dve", lambda e, o_=qin[sl][:], s_=rcol:
                  e.tensor_scalar(o_, o_, s_, MAGIC, ALU.mult, ALU.add),
                  reads=[tile, rB], writes=[tile])
            em.op("dve", lambda e, o_=qin[sl][:]:
                  e.tensor_scalar(o_, o_, MAGIC, -1.0, ALU.subtract, ALU.max),
                  reads=[tile], writes=[tile])
            em.op("dve", lambda e, o_=qout[osl][:], i_=qin[sl][:]:
                  e.tensor_scalar_min(o_, i_, 1.0),
                  reads=[tile], writes=[obB])
            dma("q_sp", dst_[t][:, s4 * 512:(s4 + 1) * 512], qout[osl][:],
                reads=[obB], writes=[B(f"w{widx}q_{t}")])
            if k % 2 == 1:
                yield

    qg = gen_quant() if stop_after == "full" else iter(())
    qa = gen_apply() if stop_after == "full" else iter(())
    qadone = [False]

    def qastep(n):
        if qadone[0]:
            return
        for _ in range(n):
            if next(qa, "END") == "END":
                qadone[0] = True
                return

    def qadrain():
        while not qadone[0]:
            qastep(64)
    qdone = [False]

    def qstep(n):
        if qdone[0]:
            return
        for _ in range(n):
            if next(qg, "END") == "END":
                qdone[0] = True
                return

    def qdrain():
        while not qdone[0]:
            qstep(64)

    xnb = [B(f"xnT{a}") for a in range(16)]
    ckvB = [B(f"ckv{rt}") for rt in range(4)]
    vB = [B(f"vall{st}") for st in range(16)]

    def ph_ln(b):
        for a in range(16):
            dma("q_pool", xnv[:, a * S:(a + 1) * S], xT[b, a], writes=[xnb[a]])
        mbk = [pbank[i] for i in range(4)]
        sbk = [pbank[4 + i] for i in range(4)]
        mB = B("ln_mean"); rB = B("ln_rinv"); tB = B("ln_tmp")
        sqB = [B("ln_sq0"), B("ln_sq1")]
        for a in range(16):
            sl = a % 2
            em.op("act", lambda e, o_=ln_sq[sl][:], i_=xnv[:, a * S:(a + 1) * S]:
                  e.activation(o_, i_, AF.Square),
                  reads=[xnb[a]], writes=[sqB[sl]])
            for nb_ in range(4):
                g = grp_of(a, 16)
                mm(psA[:, nb_ * 512:(nb_ + 1) * 512], ones_bf[:],
                   xnv[:, a * S + nb_ * 512: a * S + (nb_ + 1) * 512],
                   start=(a == 0), stop=(a == 15),
                   reads=[xnb[a], B("ones_bf")], wbank=[mbk[nb_]], grp=g)
                mm(psB[:, nb_ * 512:(nb_ + 1) * 512], ones_bf[:],
                   ln_sq[sl][:, nb_ * 512:(nb_ + 1) * 512],
                   start=(a == 0), stop=(a == 15),
                   reads=[sqB[sl], B("ones_bf")], wbank=[sbk[nb_]], grp=g)
        em.op("act", lambda e, o_=ln_mean[:], i_=psA[:, 0:2048]:
              e.activation(o_, i_, AF.Copy, scale=1.0 / 2048.0),
              reads=mbk, writes=[mB])
        em.op("act", lambda e, o_=ln_tmp[:], i_=ln_mean[:]:
              e.activation(o_, i_, AF.Square), reads=[mB], writes=[tB])
        em.op("dve", lambda e, o_=ln_rinv[:], i0=psB[:, 0:2048], i1=ln_tmp[:]:
              e.scalar_tensor_tensor(o_, i0, 1.0 / 2048.0, i1, ALU.mult,
                                     ALU.subtract),
              reads=sbk + [tB], writes=[rB])
        em.op("act", lambda e, o_=ln_rinv[:]:
              e.activation(o_, o_, AF.Ln, bias=LNEPS),
              reads=[rB, B("cscal")], writes=[rB])
        em.op("act", lambda e, o_=ln_rinv[:]:
              e.activation(o_, o_, AF.Exp, scale=-0.5), reads=[rB], writes=[rB])
        for a in range(16):
            t_ = xnv[:, a * S:(a + 1) * S]
            em.op("dve", lambda e, o_=ln_tmp[:], i0=t_, i1=ln_mean[:]:
                  e.tensor_tensor(o_, i0, i1, ALU.subtract),
                  reads=[xnb[a], mB], writes=[tB])
            em.op("dve", lambda e, o_=t_, i0=ln_tmp[:], i1=ln_rinv[:]:
                  e.tensor_tensor(o_, i0, i1, ALU.mult),
                  reads=[tB, rB], writes=[xnb[a]])
            dma("q_act", xnsp[b, a], t_, reads=[xnb[a]], writes=[B(f"xnsp{b}_{a}")])
            if "dbg_xn" in dbg_out:
                dma("q_act", dbg_out["dbg_xn"][b, a], t_, reads=[xnb[a]])

    def rope(rawv, rawBs, outv, outBs, swap_dst):
        for half in range(2):
            pst, cb, bks = swap_dst(half)
            rawB = rawBs[half]
            for j in range(2):
                em.op("pe", lambda e, o_=pst[64:128, cb + j * 512: cb + (j + 1) * 512],
                      l_=sperm[64:128, :],
                      r_=rawv[64:128, half * 1024 + j * 512: half * 1024 + (j + 1) * 512]:
                      e.matmul(o_, l_, r_, start=True, stop=True),
                      reads=[rawB, B("sperm")], writes=[bks[j]])
            c0 = half * 1024
            em.op("pool", lambda e, o_=rtmp[64:128, :],
                  i0=rawv[64:128, c0:c0 + 1024], i1=cosb[64:128, c0:c0 + 1024]:
                  e.tensor_tensor(o_, i0, i1, ALU.mult),
                  reads=[rawB, B("cosb")], writes=[B("rtmp")])
            em.op("dve", lambda e, o_=rtmp2[64:128, :],
                  i0=pst[64:128, cb:cb + 1024], i1=sinb[64:128, c0:c0 + 1024]:
                  e.tensor_tensor(o_, i0, i1, ALU.mult),
                  reads=[bks[0], bks[1], B("sinb")],
                  writes=[B("rtmp2")])
            em.op("dve", lambda e, o_=outv[64:128, c0:c0 + 1024],
                  i0=rtmp[64:128, :], i1=rtmp2[64:128, :]:
                  e.tensor_tensor(o_, i0, i1, ALU.add),
                  reads=[B("rtmp"), B("rtmp2")], writes=[outBs[half]])

    def ph_att_prologue(b):
        proB = B("prow")
        dma("q_pool", prow[:], wkvd_c.rearrange("p a t c -> p (a t c)"),
            writes=[proB])
        for rt in range(4):
            for sb_ in range(4):
                bk = pbank[sb_]
                for a in range(16):
                    mm(psA[:, sb_ * 512:(sb_ + 1) * 512],
                       prow[:, (a * 4 + rt) * 128:(a * 4 + rt) * 128 + 128],
                       xnv[:, a * S + sb_ * 512: a * S + (sb_ + 1) * 512],
                       start=(a == 0), stop=(a == 15),
                       reads=[proB, xnb[a]], wbank=[bk], grp=grp_of(a, 16))
                em.op("dve", lambda e,
                      o_=ckv[:, rt * S + sb_ * 512: rt * S + (sb_ + 1) * 512],
                      i_=psA[:, sb_ * 512:(sb_ + 1) * 512]:
                      e.tensor_copy(o_, i_), reads=[bk], writes=[ckvB[rt]],
                      grp=grp_of(sb_, 4))
            if "dbg_ckv" in dbg_out and b == 0:
                dma("q_act", dbg_out["dbg_ckv"][rt], ckv[:, rt * S:(rt + 1) * S],
                    reads=[ckvB[rt]])
        dma("q_pool", prow[:, 0:1024], wkrope_c.rearrange("p a c -> p (a c)"),
            writes=[proB])
        kb = [pbank[4 + i] for i in range(4)]
        for sb_ in range(4):
            for a in range(16):
                mm(psB[64:128, sb_ * 512:(sb_ + 1) * 512],
                   prow[:, a * 64:(a + 1) * 64],
                   xnv[:, a * S + sb_ * 512: a * S + (sb_ + 1) * 512],
                   start=(a == 0), stop=(a == 15),
                   reads=[proB, xnb[a]], wbank=[kb[sb_]], grp=grp_of(a, 16))
        em.op("act", lambda e, o_=qraw[64:128, :], i_=psB[64:128, 0:2048]:
              e.activation(o_, i_, AF.Copy), reads=kb, writes=[B("qraw")])
        if "dbg_kraw" in dbg_out and b == 0:
            dma("q_act", dbg_out["dbg_kraw"], qraw[:], reads=[B("qraw")])
        rope(qraw, [B("qraw")] * 2, krope, [B("krope")] * 2,
             lambda half: (psB, half * 1024, [kb[half * 2], kb[half * 2 + 1]]))
        if "dbg_krope" in dbg_out and b == 0:
            dma("q_act", dbg_out["dbg_krope"], krope[:], reads=[B("krope")])
        dma("q_pool", prow[:], wvup_c.rearrange("p a c -> p (a c)"), writes=[proB])
        for st in range(16):
            for ob in range(4):
                bk = pbank[ob]
                for a in range(4):
                    mm(psA[:, ob * 512:(ob + 1) * 512],
                       ckv[:, a * S + st * 128: a * S + st * 128 + 128],
                       prow[:, a * 2048 + ob * 512: a * 2048 + (ob + 1) * 512],
                       start=(a == 0), stop=(a == 3),
                       reads=[ckvB[a], proB], wbank=[bk], grp=grp_of(a, 4))
            em.op("dve", lambda e, o_=vall[:, st * 2048:(st + 1) * 2048],
                  i_=psA[:, 0:2048]:
                  e.tensor_copy(o_, i_),
                  reads=[pbank[0], pbank[1], pbank[2], pbank[3]],
                  writes=[vB[st]])
        em.op("dve", lambda e, o_=kstk[64:128, :], i_=krope[64:128, :]:
              e.tensor_copy(o_, i_), reads=[B("krope")],
              writes=[B("kstk_hi"), B("prow")])

    def ph_head(b, h):
        par = h % 2
        G1 = [pbank[i] for i in (range(4) if par == 0 else range(4, 8))]
        G2 = [pbank[i] for i in (range(4, 8) if par == 0 else range(4))]
        P1 = psA if par == 0 else psB
        P2 = psB if par == 0 else psA
        wqv = wqs_v[par]
        wqB = B(f"wqs{par}")
        wkv = wkups[:]
        wkB = B("wkups")
        wq_w = [wqB, B("prow")] if h < 2 else [wqB]
        dma("q_pool", wqv[:], wq_c[h].rearrange("p a c -> p (a c)"), writes=wq_w)
        dma("q_pool", wkv, wkup_c[h].rearrange("p a c -> p (a c)"), writes=[wkB])
        for sb_ in range(4):
            kb_ = G2[2 + sb_ % 2]
            col = 1024 + (sb_ % 2) * 512
            for a in range(4):
                mm(P2[0:64, col:col + 512],
                   wkv[:, a * 64:(a + 1) * 64],
                   ckv[:, a * S + sb_ * 512: a * S + (sb_ + 1) * 512],
                   start=(a == 0), stop=(a == 3),
                   reads=[wkB, ckvB[a]], wbank=[kb_], grp=grp_of(a, 4))
            em.op("act", lambda e, o_=kstk[0:64, sb_ * 512:(sb_ + 1) * 512],
                  i_=P2[0:64, col:col + 512]:
                  e.activation(o_, i_, AF.Copy),
                  reads=[kb_], writes=[B("kstk_lo")], grp=grp_of(sb_, 4))
        for sb_ in range(4):
            for a in range(16):
                mm(P2[:, sb_ * 512:(sb_ + 1) * 512],
                   wqv[:, a * 128:(a + 1) * 128],
                   xnv[:, a * S + sb_ * 512: a * S + (sb_ + 1) * 512],
                   start=(a == 0), stop=(a == 15),
                   reads=[wqB, xnb[a]], wbank=[G2[sb_]], grp=grp_of(a, 16))
            em.op("act", lambda e, o_=qstk[:, sb_ * 512:(sb_ + 1) * 512],
                  i_=P2[:, sb_ * 512:(sb_ + 1) * 512]:
                  e.activation(o_, i_, AF.Copy, scale=SCL),
                  reads=[G2[sb_]],
                  writes=[B("qstk_lo"), B(f"qstk_hi{sb_ // 2}")],
                  grp=grp_of(sb_ % 2, 2)),
        rope(qstk, [B("qstk_hi0"), B("qstk_hi1")], qstk,
             [B("qstk_hi0"), B("qstk_hi1")],
             lambda half: (P1, 1024, [G1[2], G1[3]]))
        qkB0 = [B("kstk_lo"), B("kstk_hi"), B("qstk_lo"), B("qstk_hi0")]
        qkB1 = [B("kstk_lo"), B("kstk_hi"), B("qstk_lo"), B("qstk_hi1")]
        ptB = [B("pts0"), B("pts1"), B("pts2")]
        OB_ = G1[2]
        RB_ = G1[3]
        slots = [(P1, 0, G1[0]), (P1, 512, G1[1]), (P2, 0, G2[0]), (P2, 512, G2[1])]
        for sb_ in range(4):
            T = 4 * sb_ + 4
            q0 = sb_ * 512

            def emit_pv(t):
                off = max(0, t - 4 * sb_) * 128
                pt = pts[:, (t % 3) * 512 + off: (t % 3) * 512 + 512]
                mm(P1[:, 2 * 512 + off: 3 * 512],
                   vall[:, t * 2048 + h * 128: t * 2048 + (h + 1) * 128],
                   pt, start=(t == 0), stop=(t == T - 1),
                   reads=[vB[t], ptB[t % 3]], wbank=[OB_], grp=grp_of(t, T))
                mm(P1[:, 3 * 512 + off: 4 * 512], ones_bf[:], pt,
                   start=(t == 0), stop=(t == T - 1),
                   reads=[B("ones_bf"), ptB[t % 3]], wbank=[RB_],
                   grp=grp_of(t, T))

            pend = []
            for t in range(T):
                qt_local = t - 4 * sb_
                off = max(0, qt_local) * 128
                pst, col, scb = slots[t % 4]
                scv = pst[:, col + off: col + 512]
                mm(scv, kstk[:, t * 128:(t + 1) * 128],
                   qstk[:, q0 + off: q0 + 512],
                   start=True, stop=True,
                   reads=(qkB0 if sb_ < 2 else qkB1), wbank=[scb])
                if qt_local >= 0:
                    em.op("dve", lambda e, o_=pst[:, col + off: col + off + 128],
                          i1=maskb[:]:
                          e.tensor_tensor(o_, o_, i1, ALU.add),
                          reads=[scb, B("maskb")], writes=[scb])
                pt = pts[:, (t % 3) * 512 + off: (t % 3) * 512 + 512]
                em.op("act", lambda e, o_=pt, i_=scv:
                      e.activation(o_, i_, AF.Exp),
                      reads=[scb], writes=[ptB[t % 3]])
                pend.append(t)
                if len(pend) > 2:
                    emit_pv(pend.pop(0))
            for t in pend:
                emit_pv(t)
            em.op("act", lambda e, o_=rln[:], i_=P1[:, 3 * 512:4 * 512]:
                  e.activation(o_, i_, AF.Ln), reads=[RB_], writes=[B("rln")])
            em.op("act", lambda e, o_=rrec[:], i_=rln[:]:
                  e.activation(o_, i_, AF.Exp, scale=-1.0),
                  reads=[B("rln")], writes=[B("rrec")])
            ov = obf[:, (sb_ % 2) * 512:(sb_ % 2) * 512 + 512]
            ovB = B(f"obf{sb_ % 2}")
            em.op("dve", lambda e, o_=ov, i0=P1[:, 2 * 512:3 * 512], i1=rrec[:]:
                  e.tensor_tensor(o_, i0, i1, ALU.mult),
                  reads=[OB_, B("rrec")], writes=[ovB])
            for j in range(4):
                qt = 4 * sb_ + j
                dma("q_act", osp[b, qt, :, h, :], ov[:, j * 128:(j + 1) * 128],
                    reads=[ovB], writes=[B(f"osp{b}_{qt}")])
                if "dbg_o" in dbg_out:
                    dma("q_act", dbg_out["dbg_o"][b, qt, :, h, :],
                        ov[:, j * 128:(j + 1) * 128], reads=[ovB])

    def ph_ffn(b):
        hB = [B(f"hid{t}") for t in range(HT)]
        xnsB = B("xns")
        w1B = [B("w1s0"), B("w1s1")]
        w2B = [B("w2s0"), B("w2s1")]
        w3B = B("w3col")
        woB = B("wocol")
        ocB = [B("ocol0"), B("ocol1")]
        xrB = [B("xres0"), B("xres1")]
        outB = [B("outb0"), B("outb1")]
        w3deps = [B(f"w2q_{t}") for t in range(HT)]
        for sb2 in range(2):
            dma("q_sp", xns[:].rearrange("p (a c) -> p a c", a=16),
                xnsp[b, :, :, sb2 * 1024:(sb2 + 1) * 1024].transpose([1, 0, 2]),
                reads=[B(f"xnsp{b}_{a}") for a in range(16)],
                writes=[xnsB, w3B])
            w1fB = [B("w1f0"), B("w1f1")]
            w2fB = [B("w2f0"), B("w2f1")]

            def quant_w12(t):
                sl = t % 2
                dma("q_sp", w1f[sl][:],
                    w1_c[t].rearrange("p a c -> p (a c)"), writes=[w1fB[sl]])
                dma("q_sp", w2f[sl][:],
                    w2_c[t].rearrange("p a c -> p (a c)"), writes=[w2fB[sl]])
                for (wf, wfB, ws, wsB, rc) in (
                        (w1f[sl], w1fB[sl], w1s[sl], w1B[sl], small[:, 4:5]),
                        (w2f[sl], w2fB[sl], w2s[sl], w2B[sl], small[:, 5:6])):
                    em.op("dve", lambda e, o_=wf[:], s_=rc:
                          e.tensor_scalar(o_, o_, s_, MAGIC, ALU.mult, ALU.add),
                          reads=[wfB, B("qrec0"), B("qrec1")], writes=[wfB])
                    em.op("dve", lambda e, o_=wf[:]:
                          e.tensor_scalar(o_, o_, MAGIC, -1.0, ALU.subtract,
                                          ALU.max),
                          reads=[wfB], writes=[wfB])
                    em.op("dve", lambda e, o_=ws[:], i_=wf[:]:
                          e.tensor_scalar_min(o_, i_, 1.0),
                          reads=[wfB], writes=[wsB])

            quant_w12(0)
            for t in range(HT):
                if t + 1 < HT:
                    quant_w12(t + 1)
                if b == 0:
                    qastep(3)
                sl = t % 2
                b0 = 0 if sl == 0 else 4
                PX = psA if sl == 0 else psB
                for a in range(16):
                    for nb_ in range(2):
                        mm(PX[:, nb_ * 512:(nb_ + 1) * 512],
                           w1s[sl][:, a * 128:(a + 1) * 128],
                           xns[:, a * 1024 + nb_ * 512: a * 1024 + (nb_ + 1) * 512],
                           start=(a == 0), stop=(a == 15),
                           reads=[w1B[sl], xnsB], wbank=[pbank[b0 + nb_]],
                           grp=grp_of(a, 16))
                for a in range(16):
                    for nb_ in range(2):
                        mm(PX[:, 1024 + nb_ * 512: 1024 + (nb_ + 1) * 512],
                           w2s[sl][:, a * 128:(a + 1) * 128],
                           xns[:, a * 1024 + nb_ * 512: a * 1024 + (nb_ + 1) * 512],
                           start=(a == 0), stop=(a == 15),
                           reads=[w2B[sl], xnsB], wbank=[pbank[b0 + 2 + nb_]],
                           grp=grp_of(a, 16))
                em.op("act", lambda e, o_=sgb[:], i_=PX[:, 0:1024]:
                      e.activation(o_, i_, AF.Sigmoid),
                      reads=[pbank[b0], pbank[b0 + 1]], writes=[B("sgb")])
                em.op("dve", lambda e, o_=hid(t)[:], i0=PX[:, 1024:2048],
                      i1=sgb[:]:
                      e.tensor_tensor(o_, i0, i1, ALU.mult),
                      reads=[pbank[b0 + 2], pbank[b0 + 3], B("sgb")],
                      writes=[hB[t]])
                if "dbg_hid" in dbg_out and b == 0 and sb2 == 0:
                    dma("q_act", dbg_out["dbg_hid"][t], hid(t)[:], reads=[hB[t]])
            if b == 0:
                qadrain()
            for ob in range(4):
                w3BA = B("w3colA")
                dma("q_sp",
                    w3col[:, 32 * 512:HT * 512].rearrange(
                        "p (h c) -> p h c", h=HT - 32),
                    w3q[32:HT, :, ob, :].transpose([1, 0, 2]),
                    reads=w3deps, writes=[w3BA, B("w1f0"), B("w1f1")])
                dma("q_sp",
                    w3col[:, 0:32 * 512].rearrange("p (h c) -> p h c", h=32),
                    w3q[0:32, :, ob, :].transpose([1, 0, 2]),
                    reads=w3deps, writes=[w3B, xnsB])
                dma("q_pool", wocol[:].rearrange("p (a c) -> p a c", a=16),
                    wo_c[:, :, ob, :],
                    writes=[woB, B("w1f1"), B("w2f0"), B("w2f1")])
                for qt in range(8):
                    qtg = sb2 * 8 + qt
                    sl = qt % 2
                    dma("q_sp", ocol[sl][:],
                        osp[b, qtg].rearrange("p d c -> p (d c)"),
                        reads=[B(f"osp{b}_{qtg}")],
                        writes=[ocB[sl]] + ([B("w2f1")] if sl == 0 else []))
                    dma("q_act", xres[sl][:],
                        xr[b, qtg * 128:(qtg + 1) * 128, ob * 512:(ob + 1) * 512],
                        writes=[xrB[sl]])
                    ybank = pbank[qt % 4]
                    yv = ps(qt % 4)
                    for hc in range(HT):
                        mm(yv, hid(hc)[:, qt * 128:(qt + 1) * 128],
                           w3col[:, hc * 512:(hc + 1) * 512],
                           start=(hc == 0), stop=False,
                           reads=[hB[hc], w3B if hc < 32 else B("w3colA")],
                           wbank=[ybank],
                           grp=("start" if hc == 0 else "mid"))
                    for dc in range(16):
                        mm(yv, ocol[sl][:, dc * 128:(dc + 1) * 128],
                           wocol[:, dc * 512:(dc + 1) * 512],
                           start=False, stop=(dc == 15),
                           reads=[ocB[sl], woB], wbank=[ybank],
                           grp=("stop" if dc == 15 else "mid"))
                    em.op("dve", lambda e, o_=outb[sl][:], i0=yv, i1=xres[sl][:]:
                          e.tensor_tensor(o_, i0, i1, ALU.add),
                          reads=[ybank, xrB[sl]], writes=[outB[sl]])
                    dma("q_act",
                        y[b, qtg * 128:(qtg + 1) * 128, ob * 512:(ob + 1) * 512],
                        outb[sl][:], reads=[outB[sl]])

    for b in range(NB):
        em.fence()
        ph_ln(b)
        if stop_after == "ln":
            continue
        em.fence()
        ph_att_prologue(b)
        em.fence()
        for h in range(H):
            ph_head(b, h)
            qstep(8)
    if stop_after == "full":
        qdrain()
        if "dbg_w1q" in dbg_out:
            for t in range(HT):
                dma("q_act", dbg_out["dbg_w3q"][t], w3q[t],
                    reads=[B(f"w2q_{t}")])
            dma("q_act", dbg_out["dbg_gam"], small[:, 0:8], reads=[B("qrec0"), B("qrec1"), B("qrec2")])
        em.fence()
        for b in range(NB):
            ph_ffn(b)

    em.wait_all("sp", QUEUES)
    em.wait_all("act", ("q_act",))
    em.wait_all("pool", ("q_pool",))

    import sys
    print("EM op counts:", em.stats(), file=sys.stderr)

    em.finalize(nc)
    with nc.Block() as block:
        block.tensor(lambda e: em.emit_engine("pe", e))
        block.scalar(lambda e: em.emit_engine("act", e))
        block.vector(lambda e: em.emit_engine("dve", e))
        block.gpsimd(lambda e: em.emit_engine("pool", e))
        block.sync(lambda e: em.emit_engine("sp", e))
    return nc


def _make_consts(wq, wkvd, wkup, wkrope, wvup, wo, w1, w2, w3):
    f = np.float32
    wq_c = np.ascontiguousarray(
        wq.T.reshape(16, 128, 16, 128).transpose(2, 1, 0, 3))
    wkvd_c = np.ascontiguousarray(
        wkvd.T.reshape(16, 128, 4, 128).transpose(1, 0, 2, 3))
    wkup_c = np.ascontiguousarray(
        wkup.T.reshape(4, 128, 16, 64).transpose(2, 1, 0, 3))
    wkrope_c = np.ascontiguousarray(
        wkrope.T.reshape(16, 128, 64).transpose(1, 0, 2))
    wvup_c = np.ascontiguousarray(
        wvup.T.reshape(4, 128, 2048).transpose(1, 0, 2))
    wo_c = np.ascontiguousarray(
        wo.T.reshape(16, 128, 4, 512).transpose(1, 0, 2, 3))
    w1_c = np.ascontiguousarray(
        w1.T.reshape(16, 128, HT, 128).transpose(2, 1, 0, 3))
    w2_c = np.ascontiguousarray(
        w2.T.reshape(16, 128, HT, 128).transpose(2, 1, 0, 3))
    w3_c = np.ascontiguousarray(w3.T.reshape(HT, 128, 4, 512))
    ones = np.ones((128, 128), f)
    Sm = np.zeros((64, 64), f)
    for i in range(32):
        Sm[2 * i, 2 * i + 1] = -1.0
        Sm[2 * i + 1, 2 * i] = 1.0
    sperm = np.ascontiguousarray(Sm.T)
    pos = np.arange(S, dtype=np.float64)
    freqs = 1.0 / (10000.0 ** (np.arange(0, RD, 2, dtype=np.float64) / RD))
    ang = pos[:, None] * freqs[None, :]
    cos = np.ascontiguousarray(np.repeat(np.cos(ang), 2, axis=1).T).astype(f)
    sin = np.ascontiguousarray(np.repeat(np.sin(ang), 2, axis=1).T).astype(f)
    kk = np.arange(128)[:, None]
    qq = np.arange(128)[None, :]
    mask = np.where(kk > qq, -1e30, 0.0).astype(f)
    return {
        "wq_c": wq_c, "wkvd_c": wkvd_c, "wkup_c": wkup_c,
        "wkrope_c": wkrope_c, "wvup_c": wvup_c, "wo_c": wo_c,
        "w1_c": w1_c, "w2_c": w2_c, "w3_c": w3_c, "ones": ones,
        "sperm": sperm, "cos": cos, "sin": sin, "mask": mask,
        "scal": np.tile(np.array([[LNEPS, MAGIC, -MAGIC]], f), (128, 1)),
    }


def _in_map(core, x, consts):
    xb = x[2 * core:2 * core + 2]
    m = {
        "xT": np.ascontiguousarray(xb.transpose(0, 2, 1)).reshape(NB, 16, 128, S),
        "xr": np.ascontiguousarray(xb),
        "c_ones": consts["ones"], "c_sperm": consts["sperm"],
        "c_cos": consts["cos"], "c_sin": consts["sin"],
        "c_mask": consts["mask"],
        "c_scal": consts["scal"],
    }
    for k in ("wq_c", "wkvd_c", "wkup_c", "wkrope_c", "wvup_c", "wo_c",
              "w1_c", "w2_c", "w3_c"):
        m[k] = consts[k]
    return m


def kernel(L0_attn_x, L0_attn_wq, L0_attn_w_kv_down, L0_attn_w_k_up,
           L0_attn_w_k_rope, L0_attn_w_v_up, L0_attn_wo, L0_ffn_w1,
           L0_ffn_w2, L0_ffn_w3):
    x = np.asarray(L0_attn_x, np.float32)
    consts = _make_consts(
        np.asarray(L0_attn_wq, np.float32),
        np.asarray(L0_attn_w_kv_down, np.float32),
        np.asarray(L0_attn_w_k_up, np.float32),
        np.asarray(L0_attn_w_k_rope, np.float32),
        np.asarray(L0_attn_w_v_up, np.float32),
        np.asarray(L0_attn_wo, np.float32),
        np.asarray(L0_ffn_w1, np.float32),
        np.asarray(L0_ffn_w2, np.float32),
        np.asarray(L0_ffn_w3, np.float32))
    nc = build_nc()
    in_maps = [_in_map(c, x, consts) for c in range(8)]
    res = run_bass_kernel_spmd(nc, in_maps, core_ids=list(range(8)))
    out = np.concatenate([np.asarray(r["y"]) for r in res.results], axis=0)
    return out.astype(np.float32)


# version 10
# speedup vs baseline: 1.0560x; 1.0560x over previous
import math
import bisect
import numpy as np
import ml_dtypes
import concourse.bass as bass
import concourse.mybir as mybir
from concourse.bass_utils import run_bass_kernel_spmd

F32 = mybir.dt.float32
BF16 = mybir.dt.bfloat16
U8 = mybir.dt.uint8
AF = mybir.ActivationFunctionType
ALU = mybir.AluOpType
AXX = mybir.AxisListType.X
BF = ml_dtypes.bfloat16

S = 2048
D = 2048
H = 16
HD = 128
R = 512
RD = 64
HID = 5632
HT = HID // 128
NB = 2
LNEPS = 1e-5
QEPS = 1e-8
SCL = 1.0 / math.sqrt(HD)
MAGIC = 12582912.0

ENGS = ("pe", "act", "dve", "pool", "sp")
QUEUES = ("q_sp", "q_act", "q_pool")
Q_OF = {"q_sp": "sp", "q_act": "act", "q_pool": "pool"}
SAME_ENGINE_SYNC = True


class Buf:
    __slots__ = ("name", "w", "r", "grp")

    def __init__(self, name):
        self.name = name
        self.w = None
        self.r = []
        self.grp = False


class Em:

    def __init__(self):
        self.ops = {e: [] for e in ENGS}
        self.clock_n = {c: 0 for c in ENGS + QUEUES}
        self.need = {c: set() for c in ENGS}
        self.bufs = {}
        self.dbg_deps = []

    def B(self, name):
        b = self.bufs.get(name)
        if b is None:
            b = self.bufs[name] = Buf(name)
        return b

    def op(self, eng, fn, reads=(), writes=(), dma=False, grp=None):
        clock = ("q_" + eng) if dma else eng
        onum = self.clock_n[clock]
        deps = []
        for b in reads:
            assert not b.grp, f"read of open accum group {b.name}"
            if b.w is not None:
                deps.append(b.w)
        first = grp in (None, "start")
        for b in writes:
            if first:
                assert not b.grp, f"write-start on open group {b.name}"
                if b.w is not None:
                    deps.append(b.w)
                deps.extend(b.r)
            else:
                assert b.grp, f"mid/stop on closed group {b.name}"
        waits = []
        for d in deps:
            c, i = d
            if c == clock and not dma:
                if eng == "pe" or not SAME_ENGINE_SYNC:
                    continue
            if d not in waits:
                waits.append(d)
                if c in ENGS:
                    self.need[c].add(i)
        rec = (clock, onum)
        for b in reads:
            b.r.append(rec)
        for b in writes:
            if grp in (None, "stop"):
                b.w = rec
                b.r = []
                b.grp = False
            else:
                b.grp = True
        self.ops[eng].append((fn, waits, clock, onum, dma, None))
        self.dbg_deps.append((clock, onum, tuple(deps)))
        self.clock_n[clock] += 1

    def fence(self):
        snap = dict(self.clock_n)
        for e in ENGS:
            waits = []
            for c in ENGS:
                if c == e:
                    continue
                if snap[c] > 0:
                    waits.append((c, snap[c] - 1))
                    self.need[c].add(snap[c] - 1)
            for c in QUEUES:
                n = snap[c]
                for r in range(min(n, self.M)):
                    i = (n - 1 - r) // self.M * self.M + r
                    if i < n:
                        waits.append((c, i))
            self.ops[e].append((None, waits, e, None, False, None))

    def wait_all(self, eng, clocks):
        self.ops[eng].append((None, None, eng, None, False, tuple(clocks)))

    M = 16

    def finalize(self, nc):
        self.sems = {}
        self._cms = []
        for c in ENGS:
            cm = nc.semaphore("sem_" + c)
            self.sems[c] = cm.__enter__()
            self._cms.append(cm)
        self.qsems = {}
        for c in QUEUES:
            self.qsems[c] = []
            for r in range(self.M):
                cm = nc.semaphore(f"sem_{c}_{r}")
                self.qsems[c].append(cm.__enter__())
                self._cms.append(cm)
        self.incmap = {c: sorted(self.need[c]) for c in ENGS}

    def q_target(self, c, i):
        return (c, i % self.M), 16 * (i // self.M + 1)

    def queue_final_targets(self, c, n=None):
        n = self.clock_n[c] if n is None else n
        out = []
        for r in range(self.M):
            cnt = (n - r + self.M - 1) // self.M if n > r else 0
            if cnt > 0:
                out.append(((c, r), 16 * cnt))
        return out

    def _sem_of(self, key):
        if isinstance(key, tuple):
            return self.qsems[key[0]][key[1]]
        return self.sems[key]

    def emit_engine(self, ename, eng):
        observed = {}
        markset = self.need.get(ename, set())

        def wt(key, v):
            if v > observed.get(key, 0):
                eng.wait_ge(self._sem_of(key), v)
                observed[key] = v

        def resolve(c, i):
            if c in ENGS:
                return (c,), bisect.bisect_right(self.incmap[c], i)
            key, v = self.q_target(c, i)
            return key, v

        for (fn, waits, clock, onum, dma, wall) in self.ops[ename]:
            if wall is not None:
                for c in wall:
                    if c in ENGS:
                        wt(c, len(self.incmap[c]))
                    else:
                        for key, v in self.queue_final_targets(c):
                            wt(key, v)
                continue
            if waits:
                for (c, i) in waits:
                    if c in ENGS:
                        wt(c, bisect.bisect_right(self.incmap[c], i))
                    else:
                        key, v = self.q_target(c, i)
                        wt(key, v)
            if fn is None:
                continue
            if dma:
                if onum >= self.M:
                    key, v = self.q_target(clock, onum - self.M)
                    wt(key, v)
                ins = fn(eng)
                ins.then_inc(self.qsems[clock][onum % self.M], 16)
            else:
                ins = fn(eng)
                if onum in markset:
                    ins.then_inc(self.sems[clock], 1)

    def stats(self):
        return {e: len(self.ops[e]) for e in ENGS}


def build_nc(stop_after="full", dbg=()):
    nc = bass.Bass("TRN2", target_bir_lowering=False)
    em = Em()
    B = em.B

    def din(name, shape, dtype=F32):
        return nc.dram_tensor(name, shape, dtype, kind="ExternalInput").ap()

    def dout(name, shape, dtype=F32):
        return nc.dram_tensor(name, shape, dtype, kind="ExternalOutput").ap()

    def dscr(name, shape, dtype=BF16):
        return nc.dram_tensor(name, shape, dtype, kind="Internal").ap()

    xT = din("xT", [NB, 16, 128, S])
    xr = din("xr", [NB, S, D])
    wq_c = din("wq_c", [H, 128, 16, 128])
    wkvd_c = din("wkvd_c", [128, 16, 4, 128])
    wkup_c = din("wkup_c", [H, 128, 4, 64])
    wkrope_c = din("wkrope_c", [128, 16, 64])
    wvup_c = din("wvup_c", [128, 4, 2048])
    wo_c = din("wo_c", [128, 16, 4, 512])
    w1_c = din("w1_c", [HT, 128, 16, 128])
    w2_c = din("w2_c", [HT, 128, 16, 128])
    w3_c = din("w3_c", [HT, 128, 4, 512])
    c_ones = din("c_ones", [128, 128])
    c_sperm = din("c_sperm", [64, 64])
    c_cos = din("c_cos", [64, S])
    c_sin = din("c_sin", [64, S])
    c_mask = din("c_mask", [128, 128])
    c_scal = din("c_scal", [128, 3])

    y = dout("y", [NB, S, D], F32)

    xnsp = dscr("xnsp", [NB, 16, 128, S])
    osp = dscr("osp", [NB, 16, 128, 16, 128])
    w3q = dscr("w3q", [HT, 128, 4, 512])
    gsc = dscr("gsc", [3, 128, 1], F32)
    gsc2 = dscr("gsc2", [3, 1, 1], F32)

    dbg_out = {}
    if "xn" in dbg:
        dbg_out["dbg_xn"] = dout("dbg_xn", [NB, 16, 128, S], BF16)
    if "ckv" in dbg:
        dbg_out["dbg_ckv"] = dout("dbg_ckv", [4, 128, S], BF16)
    if "o" in dbg:
        dbg_out["dbg_o"] = dout("dbg_o", [NB, 16, 128, 16, 128], BF16)
    if "hid" in dbg:
        dbg_out["dbg_hid"] = dout("dbg_hid", [HT, 128, 1024], BF16)
    if "wq" in dbg:
        dbg_out["dbg_w1q"] = dout("dbg_w1q", [HT, 128, 16, 128], BF16)
        dbg_out["dbg_w2q"] = dout("dbg_w2q", [HT, 128, 16, 128], BF16)
        dbg_out["dbg_w3q"] = dout("dbg_w3q", [HT, 128, 4, 512], BF16)
        dbg_out["dbg_gam"] = dout("dbg_gam", [128, 8], F32)
    if "qk" in dbg:
        dbg_out["dbg_qstk"] = dout("dbg_qstk", [128, S], BF16)
        dbg_out["dbg_kstk"] = dout("dbg_kstk", [128, S], BF16)
        dbg_out["dbg_rr"] = dout("dbg_rr", [4, 128, 512], F32)
        dbg_out["dbg_krope"] = dout("dbg_krope", [128, S], BF16)
        dbg_out["dbg_kraw"] = dout("dbg_kraw", [128, S], BF16)

    ARENA = 212352
    arena = nc.alloc_sbuf_tensor("arena", [128, ARENA], U8).ap()
    _off = [0]

    def reg(nbytes, align=512):
        o = (_off[0] + align - 1) // align * align
        _off[0] = o + nbytes
        assert _off[0] <= ARENA, f"sbuf overflow {_off[0]}"
        return o

    def view(off, nelem, dtype):
        isz = 2 if dtype == BF16 else 4
        assert off % isz == 0
        return arena[:, off:off + nelem * isz].bitcast(dtype)

    o = reg(256); ones_bf = view(o, 128, BF16)
    o = reg(512); ones_f = view(o, 128, F32)
    o = reg(128); sperm = view(o, 64, BF16)
    o = reg(4096); cosb = view(o, 2048, BF16)
    o = reg(4096); sinb = view(o, 2048, BF16)
    o = reg(512); maskb = view(o, 128, F32)
    o = reg(512); qsum = view(o, 128, F32)
    o = reg(512); small = view(o, 128, F32)
    o = reg(512); cscal = view(o, 128, F32)
    o = reg(512); grow = view(o, 128, F32)
    o_quant = reg(8448)
    NQIN = 3
    qin = [view(o_quant + i * 2048, 512, F32) for i in range(NQIN)]
    qout = [view(o_quant + 6144, 512, BF16), view(o_quant + 7168, 512, BF16)]
    o_xn = reg(65536); xnv = view(o_xn, 16 * S, BF16)
    o_vall = reg(65536); vall = view(o_vall, 16 * 2048, BF16)
    o_ckv = reg(16384); ckv = view(o_ckv, 4 * S, BF16)
    o_krope = reg(4096); krope = view(o_krope, S, BF16)
    o_head = reg(36352)
    wqs_v = [view(o_head, 2048, BF16), view(o_head + 4096, 2048, BF16)]
    qstk = view(o_head + 8192, S, BF16)
    kstk = view(o_head + 12288, S, BF16)
    wkups = view(o_head + 16384, 512, BF16)
    qraw = view(o_head + 17408, S, BF16)
    rtmp = view(o_head + 21504, 1024, F32)
    rtmp2 = view(o_head + 25600, 1024, F32)
    pts = view(o_head + 29696, 3 * 512, BF16)
    rln = view(o_head + 32768, 512, F32)
    rrec = view(o_head + 34816, 2 * 512, F32)
    assert o_head + 34816 <= o_head + 36352

    HB = o_head
    wqs_v = [view(HB + 0, 2048, BF16), view(HB + 4096, 2048, BF16)]
    qstk = view(HB + 8192, S, BF16)
    kstk = view(HB + 12288, S, BF16)
    qraw = view(HB + 16384, S, BF16)
    rtmp = view(HB + 20480, 1024, F32)
    rtmp2 = view(HB + 24576, 1024, F32)
    pts = view(HB + 28672, 4 * 512, BF16)
    rln = view(HB + 16384, 512, F32)
    rrec = view(HB + 18432, 512, F32)
    wkups = view(HB + 35840, 256, BF16)

    obf = view(o_quant, 2 * 512, BF16)
    o_obf = reg(2048)
    obf = view(o_obf, 2 * 512, BF16)

    ln_mean = view(HB + 0, 2048, F32)
    ln_rinv = view(HB + 8192, 2048, F32)
    ln_sq = [view(HB + 16384, 2048, BF16), view(HB + 20480, 2048, BF16)]
    ln_tmp = view(HB + 24576, 2048, F32)
    prow = view(HB + 0, 8192, BF16)

    def hid(t):
        if t < 32:
            return view(o_xn + t * 2048, 1024, BF16)
        if t < 40:
            return view(o_ckv + (t - 32) * 2048, 1024, BF16)
        if t < 42:
            return view(o_krope + (t - 40) * 2048, 1024, BF16)
        return view(HB + (t - 42) * 2048, 1024, BF16)
    xns = view(o_vall, 16 * 1024, BF16)
    w1f = [view(o_vall + 32768, 2048, F32), view(o_vall + 40960, 2048, F32)]
    w2f = [view(o_vall + 49152, 2048, F32), view(o_vall + 57344, 2048, F32)]
    w3col = view(o_vall, 44 * 512, BF16)
    wocol = view(o_vall + 45056, 16 * 512, BF16)
    ocol = [view(o_vall + 61440, 2048, BF16), view(HB + 4096, 2048, BF16)]
    w1s = [view(HB + 8704, 2048, BF16), view(HB + 12800, 2048, BF16)]
    w2s = [view(HB + 16896, 2048, BF16), view(HB + 20992, 2048, BF16)]
    sgb = view(HB + 25088, 1024, BF16)
    xres = [view(HB + 27136, 512, F32), view(HB + 29184, 512, F32)]
    outb = [view(HB + 31232, 512, F32), view(HB + 33280, 512, F32)]

    psA = nc.alloc_psum_tensor("psA", [128, 2048], F32).ap()
    psB = nc.alloc_psum_tensor("psB", [128, 2048], F32).ap()

    def ps(i):
        t = psA if i < 4 else psB
        return t[:, (i % 4) * 512:((i % 4) + 1) * 512]
    pbank = [B(f"psum{i}") for i in range(8)]

    def dma(queue, out, in_, reads=(), writes=()):
        eng = Q_OF[queue]
        em.op(eng, lambda e, o_=out, i_=in_: e.dma_start(out=o_, in_=i_),
              reads=list(reads), writes=list(writes), dma=True)

    def mm(out, lhsT, rhs, start, stop, reads=(), wbank=(), grp=None):
        em.op("pe", lambda e, o_=out, l_=lhsT, r_=rhs, s1=start, s2=stop:
              e.matmul(o_, l_, r_, start=s1, stop=s2),
              reads=list(reads), writes=list(wbank), grp=grp)

    def grp_of(i, n):
        if n == 1:
            return None
        return "start" if i == 0 else ("stop" if i == n - 1 else "mid")

    dma("q_pool", ones_bf[:], c_ones, writes=[B("ones_bf")])
    dma("q_sp", ones_f[:], c_ones, writes=[B("ones_f")])
    dma("q_pool", sperm[64:128, :], c_sperm, writes=[B("sperm")])
    dma("q_pool", cosb[64:128, :], c_cos, writes=[B("cosb")])
    dma("q_pool", sinb[64:128, :], c_sin, writes=[B("sinb")])
    dma("q_sp", maskb[:], c_mask, writes=[B("maskb")])
    dma("q_sp", cscal[:, 0:3], c_scal, writes=[B("cscal")])
    nc.const_aps.register(F32, LNEPS, cscal[:, 0:1])
    nc.const_aps.register(F32, MAGIC, cscal[:, 1:2])
    nc.const_aps.register(F32, -MAGIC, cscal[:, 2:3])

    QW = [(w1_c, None), (w2_c, None), (w3_c, w3q)]

    def gen_quant():
        for widx, (wsrc, wdst) in enumerate(QW):
            src = wsrc.rearrange("t p a c -> t p (a c)")
            dst = wdst.rearrange("t p a c -> t p (a c)") if wdst is not None else None
            nchunk = HT * 4
            FOLD = 112
            qsB = B("qsum")
            qfB = B("qfold")
            acc_col = small[:, widx:widx + 1]
            accB = B(f"qaccf{widx}")
            nfold = (nchunk + FOLD - 1) // FOLD
            for f in range(nfold):
                lo, hi = f * FOLD, min((f + 1) * FOLD, nchunk)
                for k in range(lo, hi):
                    t, s4 = divmod(k, 4)
                    sl = k % NQIN
                    tile = B(f"qin{sl}")
                    dma("q_sp", qin[sl][:], src[t][:, s4 * 512:(s4 + 1) * 512],
                        writes=[tile])
                    em.op("dve", lambda e, o_=qsum[:, k - lo:k - lo + 1],
                          i_=qin[sl][:]:
                          e.reduce_sum(o_, i_, axis=AXX, apply_absolute_value=True),
                          reads=[tile], writes=[qsB], grp=grp_of(k - lo, hi - lo))
                    if (k - lo) % 2 == 1:
                        yield
                em.op("dve", lambda e, o_=qsum[:, 120 + f:121 + f],
                      i_=qsum[:, 0:hi - lo]:
                      e.reduce_sum(o_, i_, axis=AXX),
                      reads=[qsB], writes=[qfB], grp=grp_of(f, nfold))
                yield
            em.op("dve", lambda e, o_=acc_col, i_=qsum[:, 120:120 + nfold]:
                  e.reduce_sum(o_, i_, axis=AXX), reads=[qfB], writes=[accB])
            gscB = B(f"gsc{widx}")
            dma("q_sp", gsc[widx], acc_col, reads=[accB], writes=[gscB])
            dma("q_sp", grow[0:1, 0:128],
                gsc[widx].rearrange("p x -> x (p)"),
                reads=[gscB], writes=[B("grow")])
            nelem = float(HID * D)
            em.op("dve", lambda e, o_=grow[0:1, 120:121], i_=grow[0:1, 0:128]:
                  e.reduce_sum(o_, i_, axis=AXX),
                  reads=[B("grow")], writes=[B("grow")])
            em.op("dve", lambda e, o_=grow[0:1, 121:122], i_=grow[0:1, 120:121]:
                  e.tensor_scalar(o_, i_, 1.0 / nelem, QEPS, ALU.mult, ALU.add),
                  reads=[B("grow")], writes=[B("grow")])
            em.op("dve", lambda e, o_=grow[0:1, 122:123], i_=grow[0:1, 121:122]:
                  e.reciprocal(o_, i_),
                  reads=[B("grow")], writes=[B("grow")])
            gs2B = B(f"gsc2_{widx}")
            dma("q_sp", gsc2[widx], grow[0:1, 122:123],
                reads=[B("grow")], writes=[gs2B])
            rcol = small[:, 4 + widx:5 + widx]
            dma("q_sp", rcol, gsc2[widx].broadcast_to([128, 1]),
                reads=[gs2B], writes=[B(f"qrec{widx}")])
            yield
            continue
            rB = B(f"qrec{widx}")
            for k in range(nchunk):
                t, s4 = divmod(k, 4)
                sl = k % NQIN
                tile = B(f"qin{sl}")
                osl = k % 2
                obB = B(f"qout{osl}")
                dma("q_sp", qin[sl][:], src[t][:, s4 * 512:(s4 + 1) * 512],
                    writes=[tile])
                em.op("dve", lambda e, o_=qin[sl][:], s_=rcol:
                      e.tensor_scalar(o_, o_, s_, MAGIC, ALU.mult, ALU.add),
                      reads=[tile, rB], writes=[tile])
                em.op("dve", lambda e, o_=qin[sl][:]:
                      e.tensor_scalar(o_, o_, MAGIC, -1.0, ALU.subtract, ALU.max),
                      reads=[tile], writes=[tile])
                em.op("dve", lambda e, o_=qout[osl][:], i_=qin[sl][:]:
                      e.tensor_scalar_min(o_, i_, 1.0),
                      reads=[tile], writes=[obB])
                dma("q_sp", dst[t][:, s4 * 512:(s4 + 1) * 512], qout[osl][:],
                    reads=[obB], writes=[B(f"w{widx}q_{t}")])
                if k % 8 == 7:
                    yield

    def gen_apply():
        widx = 2
        wsrc, wdst = QW[widx]
        src_ = wsrc.rearrange("t p a c -> t p (a c)")
        dst_ = wdst.rearrange("t p a c -> t p (a c)")
        rcol = small[:, 4 + widx:5 + widx]
        rB = B(f"qrec{widx}")
        nchunk = HT * 4
        for k in range(nchunk):
            t, s4 = divmod(k, 4)
            sl = k % NQIN
            tile = B(f"qin{sl}")
            osl = k % 2
            obB = B(f"qout{osl}")
            dma("q_sp", qin[sl][:], src_[t][:, s4 * 512:(s4 + 1) * 512],
                writes=[tile])
            em.op("dve", lambda e, o_=qin[sl][:], s_=rcol:
                  e.tensor_scalar(o_, o_, s_, MAGIC, ALU.mult, ALU.add),
                  reads=[tile, rB], writes=[tile])
            em.op("dve", lambda e, o_=qin[sl][:]:
                  e.tensor_scalar(o_, o_, MAGIC, -1.0, ALU.subtract, ALU.max),
                  reads=[tile], writes=[tile])
            em.op("dve", lambda e, o_=qout[osl][:], i_=qin[sl][:]:
                  e.tensor_scalar_min(o_, i_, 1.0),
                  reads=[tile], writes=[obB])
            dma("q_sp", dst_[t][:, s4 * 512:(s4 + 1) * 512], qout[osl][:],
                reads=[obB], writes=[B(f"w{widx}q_{t}")])
            if k % 2 == 1:
                yield

    qg = gen_quant() if stop_after == "full" else iter(())
    qa = gen_apply() if stop_after == "full" else iter(())
    qadone = [False]

    def qastep(n):
        if qadone[0]:
            return
        for _ in range(n):
            if next(qa, "END") == "END":
                qadone[0] = True
                return

    def qadrain():
        while not qadone[0]:
            qastep(64)
    qdone = [False]

    def qstep(n):
        if qdone[0]:
            return
        for _ in range(n):
            if next(qg, "END") == "END":
                qdone[0] = True
                return

    def qdrain():
        while not qdone[0]:
            qstep(64)

    xnb = [B(f"xnT{a}") for a in range(16)]
    ckvB = [B(f"ckv{rt}") for rt in range(4)]
    vB = [B(f"vall{st}") for st in range(16)]

    def ph_ln(b):
        for a in range(16):
            dma("q_pool", xnv[:, a * S:(a + 1) * S], xT[b, a], writes=[xnb[a]])
        mbk = [pbank[i] for i in range(4)]
        sbk = [pbank[4 + i] for i in range(4)]
        mB = B("ln_mean"); rB = B("ln_rinv"); tB = B("ln_tmp")
        sqB = [B("ln_sq0"), B("ln_sq1")]
        for a in range(16):
            sl = a % 2
            em.op("act", lambda e, o_=ln_sq[sl][:], i_=xnv[:, a * S:(a + 1) * S]:
                  e.activation(o_, i_, AF.Square),
                  reads=[xnb[a]], writes=[sqB[sl]])
            for nb_ in range(4):
                g = grp_of(a, 16)
                mm(psA[:, nb_ * 512:(nb_ + 1) * 512], ones_bf[:],
                   xnv[:, a * S + nb_ * 512: a * S + (nb_ + 1) * 512],
                   start=(a == 0), stop=(a == 15),
                   reads=[xnb[a], B("ones_bf")], wbank=[mbk[nb_]], grp=g)
                mm(psB[:, nb_ * 512:(nb_ + 1) * 512], ones_bf[:],
                   ln_sq[sl][:, nb_ * 512:(nb_ + 1) * 512],
                   start=(a == 0), stop=(a == 15),
                   reads=[sqB[sl], B("ones_bf")], wbank=[sbk[nb_]], grp=g)
        em.op("act", lambda e, o_=ln_mean[:], i_=psA[:, 0:2048]:
              e.activation(o_, i_, AF.Copy, scale=1.0 / 2048.0),
              reads=mbk, writes=[mB])
        em.op("act", lambda e, o_=ln_tmp[:], i_=ln_mean[:]:
              e.activation(o_, i_, AF.Square), reads=[mB], writes=[tB])
        em.op("dve", lambda e, o_=ln_rinv[:], i0=psB[:, 0:2048], i1=ln_tmp[:]:
              e.scalar_tensor_tensor(o_, i0, 1.0 / 2048.0, i1, ALU.mult,
                                     ALU.subtract),
              reads=sbk + [tB], writes=[rB])
        em.op("act", lambda e, o_=ln_rinv[:]:
              e.activation(o_, o_, AF.Ln, bias=LNEPS),
              reads=[rB, B("cscal")], writes=[rB])
        em.op("act", lambda e, o_=ln_rinv[:]:
              e.activation(o_, o_, AF.Exp, scale=-0.5), reads=[rB], writes=[rB])
        for a in range(16):
            t_ = xnv[:, a * S:(a + 1) * S]
            em.op("dve", lambda e, o_=ln_tmp[:], i0=t_, i1=ln_mean[:]:
                  e.tensor_tensor(o_, i0, i1, ALU.subtract),
                  reads=[xnb[a], mB], writes=[tB])
            em.op("dve", lambda e, o_=t_, i0=ln_tmp[:], i1=ln_rinv[:]:
                  e.tensor_tensor(o_, i0, i1, ALU.mult),
                  reads=[tB, rB], writes=[xnb[a]])
            dma("q_act", xnsp[b, a], t_, reads=[xnb[a]], writes=[B(f"xnsp{b}_{a}")])
            if "dbg_xn" in dbg_out:
                dma("q_act", dbg_out["dbg_xn"][b, a], t_, reads=[xnb[a]])

    def rope(rawv, rawBs, outv, outBs, swap_dst):
        for half in range(2):
            pst, cb, bks = swap_dst(half)
            rawB = rawBs[half]
            for j in range(2):
                em.op("pe", lambda e, o_=pst[64:128, cb + j * 512: cb + (j + 1) * 512],
                      l_=sperm[64:128, :],
                      r_=rawv[64:128, half * 1024 + j * 512: half * 1024 + (j + 1) * 512]:
                      e.matmul(o_, l_, r_, start=True, stop=True),
                      reads=[rawB, B("sperm")], writes=[bks[j]])
            c0 = half * 1024
            em.op("pool", lambda e, o_=rtmp[64:128, :],
                  i0=rawv[64:128, c0:c0 + 1024], i1=cosb[64:128, c0:c0 + 1024]:
                  e.tensor_tensor(o_, i0, i1, ALU.mult),
                  reads=[rawB, B("cosb")], writes=[B("rtmp")])
            em.op("dve", lambda e, o_=rtmp2[64:128, :],
                  i0=pst[64:128, cb:cb + 1024], i1=sinb[64:128, c0:c0 + 1024]:
                  e.tensor_tensor(o_, i0, i1, ALU.mult),
                  reads=[bks[0], bks[1], B("sinb")],
                  writes=[B("rtmp2")])
            em.op("dve", lambda e, o_=outv[64:128, c0:c0 + 1024],
                  i0=rtmp[64:128, :], i1=rtmp2[64:128, :]:
                  e.tensor_tensor(o_, i0, i1, ALU.add),
                  reads=[B("rtmp"), B("rtmp2")], writes=[outBs[half]])

    def ph_att_prologue(b):
        proB = B("prow")
        dma("q_pool", prow[:], wkvd_c.rearrange("p a t c -> p (a t c)"),
            writes=[proB])
        for rt in range(4):
            for sb_ in range(4):
                bk = pbank[sb_]
                for a in range(16):
                    mm(psA[:, sb_ * 512:(sb_ + 1) * 512],
                       prow[:, (a * 4 + rt) * 128:(a * 4 + rt) * 128 + 128],
                       xnv[:, a * S + sb_ * 512: a * S + (sb_ + 1) * 512],
                       start=(a == 0), stop=(a == 15),
                       reads=[proB, xnb[a]], wbank=[bk], grp=grp_of(a, 16))
                em.op("dve", lambda e,
                      o_=ckv[:, rt * S + sb_ * 512: rt * S + (sb_ + 1) * 512],
                      i_=psA[:, sb_ * 512:(sb_ + 1) * 512]:
                      e.tensor_copy(o_, i_), reads=[bk], writes=[ckvB[rt]],
                      grp=grp_of(sb_, 4))
            if "dbg_ckv" in dbg_out and b == 0:
                dma("q_act", dbg_out["dbg_ckv"][rt], ckv[:, rt * S:(rt + 1) * S],
                    reads=[ckvB[rt]])
        dma("q_pool", prow[:, 0:1024], wkrope_c.rearrange("p a c -> p (a c)"),
            writes=[proB])
        kb = [pbank[4 + i] for i in range(4)]
        for sb_ in range(4):
            for a in range(16):
                mm(psB[64:128, sb_ * 512:(sb_ + 1) * 512],
                   prow[:, a * 64:(a + 1) * 64],
                   xnv[:, a * S + sb_ * 512: a * S + (sb_ + 1) * 512],
                   start=(a == 0), stop=(a == 15),
                   reads=[proB, xnb[a]], wbank=[kb[sb_]], grp=grp_of(a, 16))
        em.op("act", lambda e, o_=qraw[64:128, :], i_=psB[64:128, 0:2048]:
              e.activation(o_, i_, AF.Copy), reads=kb, writes=[B("qraw")])
        if "dbg_kraw" in dbg_out and b == 0:
            dma("q_act", dbg_out["dbg_kraw"], qraw[:], reads=[B("qraw")])
        rope(qraw, [B("qraw")] * 2, krope, [B("krope")] * 2,
             lambda half: (psB, half * 1024, [kb[half * 2], kb[half * 2 + 1]]))
        if "dbg_krope" in dbg_out and b == 0:
            dma("q_act", dbg_out["dbg_krope"], krope[:], reads=[B("krope")])
        dma("q_pool", prow[:], wvup_c.rearrange("p a c -> p (a c)"), writes=[proB])
        for st in range(16):
            for ob in range(4):
                bk = pbank[ob]
                for a in range(4):
                    mm(psA[:, ob * 512:(ob + 1) * 512],
                       ckv[:, a * S + st * 128: a * S + st * 128 + 128],
                       prow[:, a * 2048 + ob * 512: a * 2048 + (ob + 1) * 512],
                       start=(a == 0), stop=(a == 3),
                       reads=[ckvB[a], proB], wbank=[bk], grp=grp_of(a, 4))
            em.op("dve", lambda e, o_=vall[:, st * 2048:(st + 1) * 2048],
                  i_=psA[:, 0:2048]:
                  e.tensor_copy(o_, i_),
                  reads=[pbank[0], pbank[1], pbank[2], pbank[3]],
                  writes=[vB[st]])
        em.op("dve", lambda e, o_=kstk[64:128, :], i_=krope[64:128, :]:
              e.tensor_copy(o_, i_), reads=[B("krope")],
              writes=[B("kstk_hi"), B("prow")])

    def ph_head(b, h):
        par = h % 2
        G1 = [pbank[i] for i in (range(4) if par == 0 else range(4, 8))]
        G2 = [pbank[i] for i in (range(4, 8) if par == 0 else range(4))]
        P1 = psA if par == 0 else psB
        P2 = psB if par == 0 else psA
        wqv = wqs_v[par]
        wqB = B(f"wqs{par}")
        wkv = wkups[:]
        wkB = B("wkups")
        wq_w = [wqB, B("prow")] if h < 2 else [wqB]
        dma("q_pool", wqv[:], wq_c[h].rearrange("p a c -> p (a c)"), writes=wq_w)
        dma("q_pool", wkv, wkup_c[h].rearrange("p a c -> p (a c)"), writes=[wkB])
        for sb_ in range(4):
            kb_ = G2[2 + sb_ % 2]
            col = 1024 + (sb_ % 2) * 512
            for a in range(4):
                mm(P2[0:64, col:col + 512],
                   wkv[:, a * 64:(a + 1) * 64],
                   ckv[:, a * S + sb_ * 512: a * S + (sb_ + 1) * 512],
                   start=(a == 0), stop=(a == 3),
                   reads=[wkB, ckvB[a]], wbank=[kb_], grp=grp_of(a, 4))
            em.op("act", lambda e, o_=kstk[0:64, sb_ * 512:(sb_ + 1) * 512],
                  i_=P2[0:64, col:col + 512]:
                  e.activation(o_, i_, AF.Copy),
                  reads=[kb_], writes=[B("kstk_lo")], grp=grp_of(sb_, 4))
        for sb_ in range(4):
            for a in range(16):
                mm(P2[:, sb_ * 512:(sb_ + 1) * 512],
                   wqv[:, a * 128:(a + 1) * 128],
                   xnv[:, a * S + sb_ * 512: a * S + (sb_ + 1) * 512],
                   start=(a == 0), stop=(a == 15),
                   reads=[wqB, xnb[a]], wbank=[G2[sb_]], grp=grp_of(a, 16))
            em.op("dve", lambda e, o_=qstk[:, sb_ * 512:(sb_ + 1) * 512],
                  i_=P2[:, sb_ * 512:(sb_ + 1) * 512]:
                  e.tensor_scalar_mul(o_, i_, SCL),
                  reads=[G2[sb_]],
                  writes=[B("qstk_lo"), B(f"qstk_hi{sb_ // 2}")],
                  grp=grp_of(sb_ % 2, 2)),
        rope(qstk, [B("qstk_hi0"), B("qstk_hi1")], qstk,
             [B("qstk_hi0"), B("qstk_hi1")],
             lambda half: (P1, 1024, [G1[2], G1[3]]))
        qkB0 = [B("kstk_lo"), B("kstk_hi"), B("qstk_lo"), B("qstk_hi0")]
        qkB1 = [B("kstk_lo"), B("kstk_hi"), B("qstk_lo"), B("qstk_hi1")]
        ptB = [B("pts0"), B("pts1"), B("pts2"), B("pts3")]
        OB_ = G1[2]
        RB_ = G1[3]
        slots = [(P1, 0, G1[0]), (P1, 512, G1[1]), (P2, 0, G2[0]), (P2, 512, G2[1])]
        for sb_ in range(4):
            T = 4 * sb_ + 4
            q0 = sb_ * 512

            def emit_pv(t):
                off = max(0, t - 4 * sb_) * 128
                pt = pts[:, (t % 4) * 512 + off: (t % 4) * 512 + 512]
                mm(P1[:, 2 * 512 + off: 3 * 512],
                   vall[:, t * 2048 + h * 128: t * 2048 + (h + 1) * 128],
                   pt, start=(t == 0), stop=(t == T - 1),
                   reads=[vB[t], ptB[t % 4]], wbank=[OB_], grp=grp_of(t, T))
                mm(P1[:, 3 * 512 + off: 4 * 512], ones_bf[:], pt,
                   start=(t == 0), stop=(t == T - 1),
                   reads=[B("ones_bf"), ptB[t % 4]], wbank=[RB_],
                   grp=grp_of(t, T))

            pend = []
            t = 0
            while t < T:
                group = [t, t + 1] if (t % 2 == 0 and t + 1 < 4 * sb_) else [t]
                grecs = []
                for u in group:
                    qt_local = u - 4 * sb_
                    off = max(0, qt_local) * 128
                    pst, col, scb = slots[u % 4]
                    scv = pst[:, col + off: col + 512]
                    mm(scv, kstk[:, u * 128:(u + 1) * 128],
                       qstk[:, q0 + off: q0 + 512],
                       start=True, stop=True,
                       reads=(qkB0 if sb_ < 2 else qkB1), wbank=[scb])
                    if qt_local >= 0:
                        em.op("dve", lambda e, o_=pst[:, col + off: col + off + 128],
                              i1=maskb[:]:
                              e.tensor_tensor(o_, o_, i1, ALU.add),
                              reads=[scb, B("maskb")], writes=[scb])
                    grecs.append((scb, off, pst, col))
                if len(group) == 2:
                    u = group[0]
                    pst = grecs[0][2]
                    em.op("act", lambda e, o_=pts[:, (u % 4) * 512:(u % 4) * 512 + 1024],
                          i_=pst[:, 0:1024]:
                          e.activation(o_, i_, AF.Exp),
                          reads=[grecs[0][0], grecs[1][0]],
                          writes=[ptB[u % 4], ptB[(u + 1) % 4]])
                else:
                    u = group[0]
                    off = grecs[0][1]
                    pst, col = grecs[0][2], grecs[0][3]
                    em.op("act", lambda e,
                          o_=pts[:, (u % 4) * 512 + off:(u % 4) * 512 + 512],
                          i_=pst[:, col + off: col + 512]:
                          e.activation(o_, i_, AF.Exp),
                          reads=[grecs[0][0]], writes=[ptB[u % 4]])
                pend.extend(group)
                while len(pend) > 2:
                    emit_pv(pend.pop(0))
                t += len(group)
            for t in pend:
                emit_pv(t)
            em.op("act", lambda e, o_=rln[:], i_=P1[:, 3 * 512:4 * 512]:
                  e.activation(o_, i_, AF.Ln), reads=[RB_], writes=[B("rln")])
            em.op("act", lambda e, o_=rrec[:], i_=rln[:]:
                  e.activation(o_, i_, AF.Exp, scale=-1.0),
                  reads=[B("rln")], writes=[B("rrec")])
            ov = obf[:, (sb_ % 2) * 512:(sb_ % 2) * 512 + 512]
            ovB = B(f"obf{sb_ % 2}")
            em.op("dve", lambda e, o_=ov, i0=P1[:, 2 * 512:3 * 512], i1=rrec[:]:
                  e.tensor_tensor(o_, i0, i1, ALU.mult),
                  reads=[OB_, B("rrec")], writes=[ovB])
            for j in range(4):
                qt = 4 * sb_ + j
                dma("q_act", osp[b, qt, :, h, :], ov[:, j * 128:(j + 1) * 128],
                    reads=[ovB], writes=[B(f"osp{b}_{qt}")])
                if "dbg_o" in dbg_out:
                    dma("q_act", dbg_out["dbg_o"][b, qt, :, h, :],
                        ov[:, j * 128:(j + 1) * 128], reads=[ovB])

    def ph_ffn(b):
        hB = [B(f"hid{t}") for t in range(HT)]
        xnsB = B("xns")
        w1B = [B("w1s0"), B("w1s1")]
        w2B = [B("w2s0"), B("w2s1")]
        w3B = B("w3col")
        woB = B("wocol")
        ocB = [B("ocol0"), B("ocol1")]
        xrB = [B("xres0"), B("xres1")]
        outB = [B("outb0"), B("outb1")]
        w3deps = [B(f"w2q_{t}") for t in range(HT)]
        for sb2 in range(2):
            dma("q_sp", xns[:].rearrange("p (a c) -> p a c", a=16),
                xnsp[b, :, :, sb2 * 1024:(sb2 + 1) * 1024].transpose([1, 0, 2]),
                reads=[B(f"xnsp{b}_{a}") for a in range(16)],
                writes=[xnsB, w3B])
            w1fB = [B("w1f0"), B("w1f1")]
            w2fB = [B("w2f0"), B("w2f1")]

            def quant_w12(t):
                sl = t % 2
                dma("q_sp", w1f[sl][:],
                    w1_c[t].rearrange("p a c -> p (a c)"), writes=[w1fB[sl]])
                dma("q_sp", w2f[sl][:],
                    w2_c[t].rearrange("p a c -> p (a c)"), writes=[w2fB[sl]])
                for (wf, wfB, ws, wsB, rc) in (
                        (w1f[sl], w1fB[sl], w1s[sl], w1B[sl], small[:, 4:5]),
                        (w2f[sl], w2fB[sl], w2s[sl], w2B[sl], small[:, 5:6])):
                    em.op("dve", lambda e, o_=wf[:], s_=rc:
                          e.tensor_scalar(o_, o_, s_, MAGIC, ALU.mult, ALU.add),
                          reads=[wfB, B("qrec0"), B("qrec1")], writes=[wfB])
                    em.op("dve", lambda e, o_=wf[:]:
                          e.tensor_scalar(o_, o_, MAGIC, -1.0, ALU.subtract,
                                          ALU.max),
                          reads=[wfB], writes=[wfB])
                    em.op("dve", lambda e, o_=ws[:], i_=wf[:]:
                          e.tensor_scalar_min(o_, i_, 1.0),
                          reads=[wfB], writes=[wsB])

            quant_w12(0)
            for t in range(HT):
                if t + 1 < HT:
                    quant_w12(t + 1)
                if b == 0:
                    qastep(3)
                sl = t % 2
                b0 = 0 if sl == 0 else 4
                PX = psA if sl == 0 else psB
                for a in range(16):
                    for nb_ in range(2):
                        mm(PX[:, nb_ * 512:(nb_ + 1) * 512],
                           w1s[sl][:, a * 128:(a + 1) * 128],
                           xns[:, a * 1024 + nb_ * 512: a * 1024 + (nb_ + 1) * 512],
                           start=(a == 0), stop=(a == 15),
                           reads=[w1B[sl], xnsB], wbank=[pbank[b0 + nb_]],
                           grp=grp_of(a, 16))
                for a in range(16):
                    for nb_ in range(2):
                        mm(PX[:, 1024 + nb_ * 512: 1024 + (nb_ + 1) * 512],
                           w2s[sl][:, a * 128:(a + 1) * 128],
                           xns[:, a * 1024 + nb_ * 512: a * 1024 + (nb_ + 1) * 512],
                           start=(a == 0), stop=(a == 15),
                           reads=[w2B[sl], xnsB], wbank=[pbank[b0 + 2 + nb_]],
                           grp=grp_of(a, 16))
                em.op("act", lambda e, o_=sgb[:], i_=PX[:, 0:1024]:
                      e.activation(o_, i_, AF.Sigmoid),
                      reads=[pbank[b0], pbank[b0 + 1]], writes=[B("sgb")])
                em.op("dve", lambda e, o_=hid(t)[:], i0=PX[:, 1024:2048],
                      i1=sgb[:]:
                      e.tensor_tensor(o_, i0, i1, ALU.mult),
                      reads=[pbank[b0 + 2], pbank[b0 + 3], B("sgb")],
                      writes=[hB[t]])
                if "dbg_hid" in dbg_out and b == 0 and sb2 == 0:
                    dma("q_act", dbg_out["dbg_hid"][t], hid(t)[:], reads=[hB[t]])
            if b == 0:
                qadrain()
            for ob in range(4):
                w3BA = B("w3colA")
                dma("q_sp",
                    w3col[:, 32 * 512:HT * 512].rearrange(
                        "p (h c) -> p h c", h=HT - 32),
                    w3q[32:HT, :, ob, :].transpose([1, 0, 2]),
                    reads=w3deps, writes=[w3BA, B("w1f0"), B("w1f1")])
                dma("q_sp",
                    w3col[:, 0:32 * 512].rearrange("p (h c) -> p h c", h=32),
                    w3q[0:32, :, ob, :].transpose([1, 0, 2]),
                    reads=w3deps, writes=[w3B, xnsB])
                dma("q_pool", wocol[:].rearrange("p (a c) -> p a c", a=16),
                    wo_c[:, :, ob, :],
                    writes=[woB, B("w1f1"), B("w2f0"), B("w2f1")])
                for qt in range(8):
                    qtg = sb2 * 8 + qt
                    sl = qt % 2
                    dma("q_sp", ocol[sl][:],
                        osp[b, qtg].rearrange("p d c -> p (d c)"),
                        reads=[B(f"osp{b}_{qtg}")],
                        writes=[ocB[sl]] + ([B("w2f1")] if sl == 0 else []))
                    dma("q_act", xres[sl][:],
                        xr[b, qtg * 128:(qtg + 1) * 128, ob * 512:(ob + 1) * 512],
                        writes=[xrB[sl]])
                    ybank = pbank[qt % 4]
                    yv = ps(qt % 4)
                    for hc in range(HT):
                        mm(yv, hid(hc)[:, qt * 128:(qt + 1) * 128],
                           w3col[:, hc * 512:(hc + 1) * 512],
                           start=(hc == 0), stop=False,
                           reads=[hB[hc], w3B if hc < 32 else B("w3colA")],
                           wbank=[ybank],
                           grp=("start" if hc == 0 else "mid"))
                    for dc in range(16):
                        mm(yv, ocol[sl][:, dc * 128:(dc + 1) * 128],
                           wocol[:, dc * 512:(dc + 1) * 512],
                           start=False, stop=(dc == 15),
                           reads=[ocB[sl], woB], wbank=[ybank],
                           grp=("stop" if dc == 15 else "mid"))
                    em.op("dve", lambda e, o_=outb[sl][:], i0=yv, i1=xres[sl][:]:
                          e.tensor_tensor(o_, i0, i1, ALU.add),
                          reads=[ybank, xrB[sl]], writes=[outB[sl]])
                    dma("q_act",
                        y[b, qtg * 128:(qtg + 1) * 128, ob * 512:(ob + 1) * 512],
                        outb[sl][:], reads=[outB[sl]])

    for b in range(NB):
        em.fence()
        ph_ln(b)
        if stop_after == "ln":
            continue
        em.fence()
        ph_att_prologue(b)
        em.fence()
        for h in range(H):
            ph_head(b, h)
            qstep(8)
    if stop_after == "full":
        qdrain()
        if "dbg_w1q" in dbg_out:
            for t in range(HT):
                dma("q_act", dbg_out["dbg_w3q"][t], w3q[t],
                    reads=[B(f"w2q_{t}")])
            dma("q_act", dbg_out["dbg_gam"], small[:, 0:8], reads=[B("qrec0"), B("qrec1"), B("qrec2")])
        em.fence()
        for b in range(NB):
            ph_ffn(b)

    em.wait_all("sp", QUEUES)
    em.wait_all("act", ("q_act",))
    em.wait_all("pool", ("q_pool",))

    import sys
    print("EM op counts:", em.stats(), file=sys.stderr)

    em.finalize(nc)
    with nc.Block() as block:
        block.tensor(lambda e: em.emit_engine("pe", e))
        block.scalar(lambda e: em.emit_engine("act", e))
        block.vector(lambda e: em.emit_engine("dve", e))
        block.gpsimd(lambda e: em.emit_engine("pool", e))
        block.sync(lambda e: em.emit_engine("sp", e))
    return nc


def _make_consts(wq, wkvd, wkup, wkrope, wvup, wo, w1, w2, w3):
    f = np.float32
    wq_c = np.ascontiguousarray(
        wq.T.reshape(16, 128, 16, 128).transpose(2, 1, 0, 3))
    wkvd_c = np.ascontiguousarray(
        wkvd.T.reshape(16, 128, 4, 128).transpose(1, 0, 2, 3))
    wkup_c = np.ascontiguousarray(
        wkup.T.reshape(4, 128, 16, 64).transpose(2, 1, 0, 3))
    wkrope_c = np.ascontiguousarray(
        wkrope.T.reshape(16, 128, 64).transpose(1, 0, 2))
    wvup_c = np.ascontiguousarray(
        wvup.T.reshape(4, 128, 2048).transpose(1, 0, 2))
    wo_c = np.ascontiguousarray(
        wo.T.reshape(16, 128, 4, 512).transpose(1, 0, 2, 3))
    w1_c = np.ascontiguousarray(
        w1.T.reshape(16, 128, HT, 128).transpose(2, 1, 0, 3))
    w2_c = np.ascontiguousarray(
        w2.T.reshape(16, 128, HT, 128).transpose(2, 1, 0, 3))
    w3_c = np.ascontiguousarray(w3.T.reshape(HT, 128, 4, 512))
    ones = np.ones((128, 128), f)
    Sm = np.zeros((64, 64), f)
    for i in range(32):
        Sm[2 * i, 2 * i + 1] = -1.0
        Sm[2 * i + 1, 2 * i] = 1.0
    sperm = np.ascontiguousarray(Sm.T)
    pos = np.arange(S, dtype=np.float64)
    freqs = 1.0 / (10000.0 ** (np.arange(0, RD, 2, dtype=np.float64) / RD))
    ang = pos[:, None] * freqs[None, :]
    cos = np.ascontiguousarray(np.repeat(np.cos(ang), 2, axis=1).T).astype(f)
    sin = np.ascontiguousarray(np.repeat(np.sin(ang), 2, axis=1).T).astype(f)
    kk = np.arange(128)[:, None]
    qq = np.arange(128)[None, :]
    mask = np.where(kk > qq, -1e30, 0.0).astype(f)
    return {
        "wq_c": wq_c, "wkvd_c": wkvd_c, "wkup_c": wkup_c,
        "wkrope_c": wkrope_c, "wvup_c": wvup_c, "wo_c": wo_c,
        "w1_c": w1_c, "w2_c": w2_c, "w3_c": w3_c, "ones": ones,
        "sperm": sperm, "cos": cos, "sin": sin, "mask": mask,
        "scal": np.tile(np.array([[LNEPS, MAGIC, -MAGIC]], f), (128, 1)),
    }


def _in_map(core, x, consts):
    xb = x[2 * core:2 * core + 2]
    m = {
        "xT": np.ascontiguousarray(xb.transpose(0, 2, 1)).reshape(NB, 16, 128, S),
        "xr": np.ascontiguousarray(xb),
        "c_ones": consts["ones"], "c_sperm": consts["sperm"],
        "c_cos": consts["cos"], "c_sin": consts["sin"],
        "c_mask": consts["mask"],
        "c_scal": consts["scal"],
    }
    for k in ("wq_c", "wkvd_c", "wkup_c", "wkrope_c", "wvup_c", "wo_c",
              "w1_c", "w2_c", "w3_c"):
        m[k] = consts[k]
    return m


def kernel(L0_attn_x, L0_attn_wq, L0_attn_w_kv_down, L0_attn_w_k_up,
           L0_attn_w_k_rope, L0_attn_w_v_up, L0_attn_wo, L0_ffn_w1,
           L0_ffn_w2, L0_ffn_w3):
    x = np.asarray(L0_attn_x, np.float32)
    consts = _make_consts(
        np.asarray(L0_attn_wq, np.float32),
        np.asarray(L0_attn_w_kv_down, np.float32),
        np.asarray(L0_attn_w_k_up, np.float32),
        np.asarray(L0_attn_w_k_rope, np.float32),
        np.asarray(L0_attn_w_v_up, np.float32),
        np.asarray(L0_attn_wo, np.float32),
        np.asarray(L0_ffn_w1, np.float32),
        np.asarray(L0_ffn_w2, np.float32),
        np.asarray(L0_ffn_w3, np.float32))
    nc = build_nc()
    in_maps = [_in_map(c, x, consts) for c in range(8)]
    res = run_bass_kernel_spmd(nc, in_maps, core_ids=list(range(8)))
    out = np.concatenate([np.asarray(r["y"]) for r in res.results], axis=0)
    return out.astype(np.float32)


# version 11 (final)
# speedup vs baseline: 1.0885x; 1.0307x over previous
import math
import bisect
import numpy as np
import ml_dtypes
import concourse.bass as bass
import concourse.mybir as mybir
from concourse.bass_utils import run_bass_kernel_spmd

F32 = mybir.dt.float32
BF16 = mybir.dt.bfloat16
U8 = mybir.dt.uint8
AF = mybir.ActivationFunctionType
ALU = mybir.AluOpType
AXX = mybir.AxisListType.X
BF = ml_dtypes.bfloat16

S = 2048
D = 2048
H = 16
HD = 128
R = 512
RD = 64
HID = 5632
HT = HID // 128
NB = 2
LNEPS = 1e-5
QEPS = 1e-8
SCL = 1.0 / math.sqrt(HD)
MAGIC = 12582912.0

ENGS = ("pe", "act", "dve", "pool", "sp")
QUEUES = ("q_sp", "q_act", "q_pool")
Q_OF = {"q_sp": "sp", "q_act": "act", "q_pool": "pool"}
SAME_ENGINE_SYNC = True


class Buf:
    __slots__ = ("name", "w", "r", "grp")

    def __init__(self, name):
        self.name = name
        self.w = None
        self.r = []
        self.grp = False


class Em:

    def __init__(self):
        self.ops = {e: [] for e in ENGS}
        self.clock_n = {c: 0 for c in ENGS + QUEUES}
        self.need = {c: set() for c in ENGS}
        self.bufs = {}
        self.dbg_deps = []

    def B(self, name):
        b = self.bufs.get(name)
        if b is None:
            b = self.bufs[name] = Buf(name)
        return b

    def op(self, eng, fn, reads=(), writes=(), dma=False, grp=None):
        clock = ("q_" + eng) if dma else eng
        onum = self.clock_n[clock]
        deps = []
        for b in reads:
            assert not b.grp, f"read of open accum group {b.name}"
            if b.w is not None:
                deps.append(b.w)
        first = grp in (None, "start")
        for b in writes:
            if first:
                assert not b.grp, f"write-start on open group {b.name}"
                if b.w is not None:
                    deps.append(b.w)
                deps.extend(b.r)
            else:
                assert b.grp, f"mid/stop on closed group {b.name}"
        waits = []
        for d in deps:
            c, i = d
            if c == clock and not dma:
                if eng == "pe" or not SAME_ENGINE_SYNC:
                    continue
            if d not in waits:
                waits.append(d)
                if c in ENGS:
                    self.need[c].add(i)
        rec = (clock, onum)
        for b in reads:
            b.r.append(rec)
        for b in writes:
            if grp in (None, "stop"):
                b.w = rec
                b.r = []
                b.grp = False
            else:
                b.grp = True
        self.ops[eng].append((fn, waits, clock, onum, dma, None))
        self.dbg_deps.append((clock, onum, tuple(deps)))
        self.clock_n[clock] += 1

    def fence(self):
        snap = dict(self.clock_n)
        for e in ENGS:
            waits = []
            for c in ENGS:
                if c == e:
                    continue
                if snap[c] > 0:
                    waits.append((c, snap[c] - 1))
                    self.need[c].add(snap[c] - 1)
            for c in QUEUES:
                n = snap[c]
                for r in range(min(n, self.M)):
                    i = (n - 1 - r) // self.M * self.M + r
                    if i < n:
                        waits.append((c, i))
            self.ops[e].append((None, waits, e, None, False, None))

    def wait_all(self, eng, clocks):
        self.ops[eng].append((None, None, eng, None, False, tuple(clocks)))

    M = 16

    def finalize(self, nc):
        self.sems = {}
        self._cms = []
        for c in ENGS:
            cm = nc.semaphore("sem_" + c)
            self.sems[c] = cm.__enter__()
            self._cms.append(cm)
        self.qsems = {}
        for c in QUEUES:
            self.qsems[c] = []
            for r in range(self.M):
                cm = nc.semaphore(f"sem_{c}_{r}")
                self.qsems[c].append(cm.__enter__())
                self._cms.append(cm)
        self.incmap = {c: sorted(self.need[c]) for c in ENGS}

    def q_target(self, c, i):
        return (c, i % self.M), 16 * (i // self.M + 1)

    def queue_final_targets(self, c, n=None):
        n = self.clock_n[c] if n is None else n
        out = []
        for r in range(self.M):
            cnt = (n - r + self.M - 1) // self.M if n > r else 0
            if cnt > 0:
                out.append(((c, r), 16 * cnt))
        return out

    def _sem_of(self, key):
        if isinstance(key, tuple):
            return self.qsems[key[0]][key[1]]
        return self.sems[key]

    def emit_engine(self, ename, eng):
        observed = {}
        markset = self.need.get(ename, set())

        def wt(key, v):
            if v > observed.get(key, 0):
                eng.wait_ge(self._sem_of(key), v)
                observed[key] = v

        def resolve(c, i):
            if c in ENGS:
                return (c,), bisect.bisect_right(self.incmap[c], i)
            key, v = self.q_target(c, i)
            return key, v

        for (fn, waits, clock, onum, dma, wall) in self.ops[ename]:
            if wall is not None:
                for c in wall:
                    if c in ENGS:
                        wt(c, len(self.incmap[c]))
                    else:
                        for key, v in self.queue_final_targets(c):
                            wt(key, v)
                continue
            if waits:
                for (c, i) in waits:
                    if c in ENGS:
                        wt(c, bisect.bisect_right(self.incmap[c], i))
                    else:
                        key, v = self.q_target(c, i)
                        wt(key, v)
            if fn is None:
                continue
            if dma:
                if onum >= self.M:
                    key, v = self.q_target(clock, onum - self.M)
                    wt(key, v)
                ins = fn(eng)
                ins.then_inc(self.qsems[clock][onum % self.M], 16)
            else:
                ins = fn(eng)
                if onum in markset:
                    ins.then_inc(self.sems[clock], 1)

    def stats(self):
        return {e: len(self.ops[e]) for e in ENGS}


def build_nc(stop_after="full", dbg=()):
    nc = bass.Bass("TRN2", target_bir_lowering=False)
    em = Em()
    B = em.B

    def din(name, shape, dtype=F32):
        return nc.dram_tensor(name, shape, dtype, kind="ExternalInput").ap()

    def dout(name, shape, dtype=F32):
        return nc.dram_tensor(name, shape, dtype, kind="ExternalOutput").ap()

    def dscr(name, shape, dtype=BF16):
        return nc.dram_tensor(name, shape, dtype, kind="Internal").ap()

    xT = din("xT", [NB, 16, 128, S])
    xr = din("xr", [NB, S, D])
    wq_c = din("wq_c", [H, 128, 16, 128])
    wkvd_c = din("wkvd_c", [128, 16, 4, 128])
    wkup_c = din("wkup_c", [H, 128, 4, 64])
    wkrope_c = din("wkrope_c", [128, 16, 64])
    wvup_c = din("wvup_c", [128, 4, 2048])
    wo_c = din("wo_c", [128, 16, 4, 512])
    w1_c = din("w1_c", [HT, 128, 16, 128])
    w2_c = din("w2_c", [HT, 128, 16, 128])
    w3_c = din("w3_c", [HT, 128, 4, 512])
    c_ones = din("c_ones", [128, 128])
    c_sperm = din("c_sperm", [64, 64])
    c_cos = din("c_cos", [64, S])
    c_sin = din("c_sin", [64, S])
    c_mask = din("c_mask", [128, 128])
    c_scal = din("c_scal", [128, 3])

    y = dout("y", [NB, S, D], F32)

    xnsp = dscr("xnsp", [NB, 16, 128, S])
    osp = dscr("osp", [NB, 16, 128, 16, 128])
    w3q = dscr("w3q", [HT, 128, 4, 512])
    gsc = dscr("gsc", [3, 128, 1], F32)
    gsc2 = dscr("gsc2", [3, 1, 1], F32)

    dbg_out = {}
    if "xn" in dbg:
        dbg_out["dbg_xn"] = dout("dbg_xn", [NB, 16, 128, S], BF16)
    if "ckv" in dbg:
        dbg_out["dbg_ckv"] = dout("dbg_ckv", [4, 128, S], BF16)
    if "o" in dbg:
        dbg_out["dbg_o"] = dout("dbg_o", [NB, 16, 128, 16, 128], BF16)
    if "hid" in dbg:
        dbg_out["dbg_hid"] = dout("dbg_hid", [HT, 128, 1024], BF16)
    if "wq" in dbg:
        dbg_out["dbg_w1q"] = dout("dbg_w1q", [HT, 128, 16, 128], BF16)
        dbg_out["dbg_w2q"] = dout("dbg_w2q", [HT, 128, 16, 128], BF16)
        dbg_out["dbg_w3q"] = dout("dbg_w3q", [HT, 128, 4, 512], BF16)
        dbg_out["dbg_gam"] = dout("dbg_gam", [128, 8], F32)
    if "qk" in dbg:
        dbg_out["dbg_qstk"] = dout("dbg_qstk", [128, S], BF16)
        dbg_out["dbg_kstk"] = dout("dbg_kstk", [128, S], BF16)
        dbg_out["dbg_rr"] = dout("dbg_rr", [4, 128, 512], F32)
        dbg_out["dbg_krope"] = dout("dbg_krope", [128, S], BF16)
        dbg_out["dbg_kraw"] = dout("dbg_kraw", [128, S], BF16)

    ARENA = 212352
    arena = nc.alloc_sbuf_tensor("arena", [128, ARENA], U8).ap()
    _off = [0]

    def reg(nbytes, align=512):
        o = (_off[0] + align - 1) // align * align
        _off[0] = o + nbytes
        assert _off[0] <= ARENA, f"sbuf overflow {_off[0]}"
        return o

    def view(off, nelem, dtype):
        isz = 2 if dtype == BF16 else 4
        assert off % isz == 0
        return arena[:, off:off + nelem * isz].bitcast(dtype)

    o = reg(256); ones_bf = view(o, 128, BF16)
    o = reg(512); ones_f = view(o, 128, F32)
    o = reg(128); sperm = view(o, 64, BF16)
    o = reg(4096); cosb = view(o, 2048, BF16)
    o = reg(4096); sinb = view(o, 2048, BF16)
    o = reg(512); maskb = view(o, 128, F32)
    o = reg(512); qsum = view(o, 128, F32)
    o = reg(512); small = view(o, 128, F32)
    o = reg(512); cscal = view(o, 128, F32)
    o = reg(512); grow = view(o, 128, F32)
    o_quant = reg(8448)
    NQIN = 3
    qin = [view(o_quant + i * 2048, 512, F32) for i in range(NQIN)]
    qout = [view(o_quant + 6144, 512, BF16), view(o_quant + 7168, 512, BF16)]
    o_xn = reg(65536); xnv = view(o_xn, 16 * S, BF16)
    o_vall = reg(65536); vall = view(o_vall, 16 * 2048, BF16)
    o_ckv = reg(16384); ckv = view(o_ckv, 4 * S, BF16)
    o_krope = reg(4096); krope = view(o_krope, S, BF16)
    o_head = reg(36352)
    wqs_v = [view(o_head, 2048, BF16), view(o_head + 4096, 2048, BF16)]
    qstk = view(o_head + 8192, S, BF16)
    kstk = view(o_head + 12288, S, BF16)
    wkups = view(o_head + 16384, 512, BF16)
    qraw = view(o_head + 17408, S, BF16)
    rtmp = view(o_head + 21504, 1024, F32)
    rtmp2 = view(o_head + 25600, 1024, F32)
    pts = view(o_head + 29696, 3 * 512, BF16)
    rln = view(o_head + 32768, 512, F32)
    rrec = view(o_head + 34816, 2 * 512, F32)
    assert o_head + 34816 <= o_head + 36352

    HB = o_head
    wqs_v = [view(HB + 0, 2048, BF16), view(HB + 4096, 2048, BF16)]
    qstk = view(HB + 8192, S, BF16)
    kstk = view(HB + 12288, S, BF16)
    qraw = view(HB + 16384, S, BF16)
    rtmp = view(HB + 20480, 1024, F32)
    rtmp2 = view(HB + 24576, 1024, F32)
    pts = view(HB + 28672, 3 * 512, BF16)
    rln = view(HB + 31744, 512, F32)
    rrec = view(HB + 33792, 512, F32)
    wkups = view(HB + 35840, 256, BF16)

    obf = view(o_quant, 2 * 512, BF16)
    o_obf = reg(2048)
    obf = view(o_obf, 2 * 512, BF16)

    ln_mean = view(HB + 0, 2048, F32)
    ln_rinv = view(HB + 8192, 2048, F32)
    ln_sq = [view(HB + 16384, 2048, BF16), view(HB + 20480, 2048, BF16)]
    ln_tmp = view(HB + 24576, 2048, F32)
    prow = view(HB + 0, 8192, BF16)

    def hid(t):
        if t < 32:
            return view(o_xn + t * 2048, 1024, BF16)
        if t < 40:
            return view(o_ckv + (t - 32) * 2048, 1024, BF16)
        if t < 42:
            return view(o_krope + (t - 40) * 2048, 1024, BF16)
        return view(HB + (t - 42) * 2048, 1024, BF16)
    xns = view(o_vall, 16 * 1024, BF16)
    w1f = [view(o_vall + 32768, 2048, F32), view(o_vall + 40960, 2048, F32)]
    w2f = [view(o_vall + 49152, 2048, F32), view(o_vall + 57344, 2048, F32)]
    w3col = view(o_vall, 44 * 512, BF16)
    wocol = view(o_vall + 45056, 16 * 512, BF16)
    ocol = [view(o_vall + 61440, 2048, BF16), view(HB + 4096, 2048, BF16)]
    w1s = [view(HB + 8704, 2048, BF16), view(HB + 12800, 2048, BF16)]
    w2s = [view(HB + 16896, 2048, BF16), view(HB + 20992, 2048, BF16)]
    sgb = view(HB + 25088, 1024, BF16)
    xres = [view(HB + 27136, 512, F32), view(HB + 29184, 512, F32)]
    outb = [view(HB + 31232, 512, F32), view(HB + 33280, 512, F32)]

    psA = nc.alloc_psum_tensor("psA", [128, 2048], F32).ap()
    psB = nc.alloc_psum_tensor("psB", [128, 2048], F32).ap()

    def ps(i):
        t = psA if i < 4 else psB
        return t[:, (i % 4) * 512:((i % 4) + 1) * 512]
    pbank = [B(f"psum{i}") for i in range(8)]

    def dma(queue, out, in_, reads=(), writes=()):
        eng = Q_OF[queue]
        em.op(eng, lambda e, o_=out, i_=in_: e.dma_start(out=o_, in_=i_),
              reads=list(reads), writes=list(writes), dma=True)

    def mm(out, lhsT, rhs, start, stop, reads=(), wbank=(), grp=None):
        em.op("pe", lambda e, o_=out, l_=lhsT, r_=rhs, s1=start, s2=stop:
              e.matmul(o_, l_, r_, start=s1, stop=s2),
              reads=list(reads), writes=list(wbank), grp=grp)

    def grp_of(i, n):
        if n == 1:
            return None
        return "start" if i == 0 else ("stop" if i == n - 1 else "mid")

    dma("q_pool", ones_bf[:], c_ones, writes=[B("ones_bf")])
    dma("q_sp", ones_f[:], c_ones, writes=[B("ones_f")])
    dma("q_pool", sperm[64:128, :], c_sperm, writes=[B("sperm")])
    dma("q_pool", cosb[64:128, :], c_cos, writes=[B("cosb")])
    dma("q_pool", sinb[64:128, :], c_sin, writes=[B("sinb")])
    dma("q_sp", maskb[:], c_mask, writes=[B("maskb")])
    dma("q_sp", cscal[:, 0:3], c_scal, writes=[B("cscal")])
    nc.const_aps.register(F32, LNEPS, cscal[:, 0:1])
    nc.const_aps.register(F32, MAGIC, cscal[:, 1:2])
    nc.const_aps.register(F32, -MAGIC, cscal[:, 2:3])

    QW = [(w1_c, None), (w2_c, None), (w3_c, w3q)]

    def gen_quant():
        for widx, (wsrc, wdst) in enumerate(QW):
            src = wsrc.rearrange("t p a c -> t p (a c)")
            dst = wdst.rearrange("t p a c -> t p (a c)") if wdst is not None else None
            nchunk = HT * 4
            FOLD = 112
            qsB = B("qsum")
            qfB = B("qfold")
            acc_col = small[:, widx:widx + 1]
            accB = B(f"qaccf{widx}")
            nfold = (nchunk + FOLD - 1) // FOLD
            for f in range(nfold):
                lo, hi = f * FOLD, min((f + 1) * FOLD, nchunk)
                for k in range(lo, hi):
                    t, s4 = divmod(k, 4)
                    sl = k % NQIN
                    tile = B(f"qin{sl}")
                    dma("q_sp", qin[sl][:], src[t][:, s4 * 512:(s4 + 1) * 512],
                        writes=[tile])
                    em.op("dve", lambda e, o_=qsum[:, k - lo:k - lo + 1],
                          i_=qin[sl][:]:
                          e.reduce_sum(o_, i_, axis=AXX, apply_absolute_value=True),
                          reads=[tile], writes=[qsB], grp=grp_of(k - lo, hi - lo))
                    if (k - lo) % 2 == 1:
                        yield
                em.op("dve", lambda e, o_=qsum[:, 120 + f:121 + f],
                      i_=qsum[:, 0:hi - lo]:
                      e.reduce_sum(o_, i_, axis=AXX),
                      reads=[qsB], writes=[qfB], grp=grp_of(f, nfold))
                yield
            em.op("dve", lambda e, o_=acc_col, i_=qsum[:, 120:120 + nfold]:
                  e.reduce_sum(o_, i_, axis=AXX), reads=[qfB], writes=[accB])
            gscB = B(f"gsc{widx}")
            dma("q_sp", gsc[widx], acc_col, reads=[accB], writes=[gscB])
            dma("q_sp", grow[0:1, 0:128],
                gsc[widx].rearrange("p x -> x (p)"),
                reads=[gscB], writes=[B("grow")])
            nelem = float(HID * D)
            em.op("dve", lambda e, o_=grow[0:1, 120:121], i_=grow[0:1, 0:128]:
                  e.reduce_sum(o_, i_, axis=AXX),
                  reads=[B("grow")], writes=[B("grow")])
            em.op("dve", lambda e, o_=grow[0:1, 121:122], i_=grow[0:1, 120:121]:
                  e.tensor_scalar(o_, i_, 1.0 / nelem, QEPS, ALU.mult, ALU.add),
                  reads=[B("grow")], writes=[B("grow")])
            em.op("dve", lambda e, o_=grow[0:1, 122:123], i_=grow[0:1, 121:122]:
                  e.reciprocal(o_, i_),
                  reads=[B("grow")], writes=[B("grow")])
            gs2B = B(f"gsc2_{widx}")
            dma("q_sp", gsc2[widx], grow[0:1, 122:123],
                reads=[B("grow")], writes=[gs2B])
            rcol = small[:, 4 + widx:5 + widx]
            dma("q_sp", rcol, gsc2[widx].broadcast_to([128, 1]),
                reads=[gs2B], writes=[B(f"qrec{widx}")])
            yield
            continue
            rB = B(f"qrec{widx}")
            for k in range(nchunk):
                t, s4 = divmod(k, 4)
                sl = k % NQIN
                tile = B(f"qin{sl}")
                osl = k % 2
                obB = B(f"qout{osl}")
                dma("q_sp", qin[sl][:], src[t][:, s4 * 512:(s4 + 1) * 512],
                    writes=[tile])
                em.op("dve", lambda e, o_=qin[sl][:], s_=rcol:
                      e.tensor_scalar(o_, o_, s_, MAGIC, ALU.mult, ALU.add),
                      reads=[tile, rB], writes=[tile])
                em.op("dve", lambda e, o_=qin[sl][:]:
                      e.tensor_scalar(o_, o_, MAGIC, -1.0, ALU.subtract, ALU.max),
                      reads=[tile], writes=[tile])
                em.op("dve", lambda e, o_=qout[osl][:], i_=qin[sl][:]:
                      e.tensor_scalar_min(o_, i_, 1.0),
                      reads=[tile], writes=[obB])
                dma("q_sp", dst[t][:, s4 * 512:(s4 + 1) * 512], qout[osl][:],
                    reads=[obB], writes=[B(f"w{widx}q_{t}")])
                if k % 8 == 7:
                    yield

    def gen_apply():
        widx = 2
        wsrc, wdst = QW[widx]
        src_ = wsrc.rearrange("t p a c -> t p (a c)")
        dst_ = wdst.rearrange("t p a c -> t p (a c)")
        rcol = small[:, 4 + widx:5 + widx]
        rB = B(f"qrec{widx}")
        nchunk = HT * 4
        for k in range(nchunk):
            t, s4 = divmod(k, 4)
            sl = k % NQIN
            tile = B(f"qin{sl}")
            osl = k % 2
            obB = B(f"qout{osl}")
            dma("q_sp", qin[sl][:], src_[t][:, s4 * 512:(s4 + 1) * 512],
                writes=[tile])
            em.op("dve", lambda e, o_=qin[sl][:], s_=rcol:
                  e.tensor_scalar(o_, o_, s_, MAGIC, ALU.mult, ALU.add),
                  reads=[tile, rB], writes=[tile])
            em.op("dve", lambda e, o_=qin[sl][:]:
                  e.tensor_scalar(o_, o_, MAGIC, -1.0, ALU.subtract, ALU.max),
                  reads=[tile], writes=[tile])
            em.op("dve", lambda e, o_=qout[osl][:], i_=qin[sl][:]:
                  e.tensor_scalar_min(o_, i_, 1.0),
                  reads=[tile], writes=[obB])
            dma("q_sp", dst_[t][:, s4 * 512:(s4 + 1) * 512], qout[osl][:],
                reads=[obB], writes=[B(f"w{widx}q_{t}")])
            if k % 2 == 1:
                yield

    qg = gen_quant() if stop_after == "full" else iter(())
    qa = gen_apply() if stop_after == "full" else iter(())
    qadone = [False]

    def qastep(n):
        if qadone[0]:
            return
        for _ in range(n):
            if next(qa, "END") == "END":
                qadone[0] = True
                return

    def qadrain():
        while not qadone[0]:
            qastep(64)
    qdone = [False]

    def qstep(n):
        if qdone[0]:
            return
        for _ in range(n):
            if next(qg, "END") == "END":
                qdone[0] = True
                return

    def qdrain():
        while not qdone[0]:
            qstep(64)

    xnb = [B(f"xnT{a}") for a in range(16)]
    ckvB = [B(f"ckv{rt}") for rt in range(4)]
    vB = [B(f"vall{st}") for st in range(16)]

    def ph_ln(b):
        for a in range(16):
            dma("q_pool", xnv[:, a * S:(a + 1) * S], xT[b, a], writes=[xnb[a]])
        mbk = [pbank[i] for i in range(4)]
        sbk = [pbank[4 + i] for i in range(4)]
        mB = B("ln_mean"); rB = B("ln_rinv"); tB = B("ln_tmp")
        sqB = [B("ln_sq0"), B("ln_sq1")]
        for a in range(16):
            sl = a % 2
            em.op("act", lambda e, o_=ln_sq[sl][:], i_=xnv[:, a * S:(a + 1) * S]:
                  e.activation(o_, i_, AF.Square),
                  reads=[xnb[a]], writes=[sqB[sl]])
            for nb_ in range(4):
                g = grp_of(a, 16)
                mm(psA[:, nb_ * 512:(nb_ + 1) * 512], ones_bf[:],
                   xnv[:, a * S + nb_ * 512: a * S + (nb_ + 1) * 512],
                   start=(a == 0), stop=(a == 15),
                   reads=[xnb[a], B("ones_bf")], wbank=[mbk[nb_]], grp=g)
                mm(psB[:, nb_ * 512:(nb_ + 1) * 512], ones_bf[:],
                   ln_sq[sl][:, nb_ * 512:(nb_ + 1) * 512],
                   start=(a == 0), stop=(a == 15),
                   reads=[sqB[sl], B("ones_bf")], wbank=[sbk[nb_]], grp=g)
        em.op("act", lambda e, o_=ln_mean[:], i_=psA[:, 0:2048]:
              e.activation(o_, i_, AF.Copy, scale=1.0 / 2048.0),
              reads=mbk, writes=[mB])
        em.op("act", lambda e, o_=ln_tmp[:], i_=ln_mean[:]:
              e.activation(o_, i_, AF.Square), reads=[mB], writes=[tB])
        em.op("dve", lambda e, o_=ln_rinv[:], i0=psB[:, 0:2048], i1=ln_tmp[:]:
              e.scalar_tensor_tensor(o_, i0, 1.0 / 2048.0, i1, ALU.mult,
                                     ALU.subtract),
              reads=sbk + [tB], writes=[rB])
        em.op("act", lambda e, o_=ln_rinv[:]:
              e.activation(o_, o_, AF.Ln, bias=LNEPS),
              reads=[rB, B("cscal")], writes=[rB])
        em.op("act", lambda e, o_=ln_rinv[:]:
              e.activation(o_, o_, AF.Exp, scale=-0.5), reads=[rB], writes=[rB])
        for a in range(16):
            t_ = xnv[:, a * S:(a + 1) * S]
            em.op("dve", lambda e, o_=ln_tmp[:], i0=t_, i1=ln_mean[:]:
                  e.tensor_tensor(o_, i0, i1, ALU.subtract),
                  reads=[xnb[a], mB], writes=[tB])
            em.op("dve", lambda e, o_=t_, i0=ln_tmp[:], i1=ln_rinv[:]:
                  e.tensor_tensor(o_, i0, i1, ALU.mult),
                  reads=[tB, rB], writes=[xnb[a]])
            dma("q_act", xnsp[b, a], t_, reads=[xnb[a]], writes=[B(f"xnsp{b}_{a}")])
            if "dbg_xn" in dbg_out:
                dma("q_act", dbg_out["dbg_xn"][b, a], t_, reads=[xnb[a]])

    def rope(rawv, rawB, outv, outB, swap_dst):
        for half in range(2):
            pst, cb, bks = swap_dst(half)
            for j in range(2):
                em.op("pe", lambda e, o_=pst[64:128, cb + j * 512: cb + (j + 1) * 512],
                      l_=sperm[64:128, :],
                      r_=rawv[64:128, half * 1024 + j * 512: half * 1024 + (j + 1) * 512]:
                      e.matmul(o_, l_, r_, start=True, stop=True),
                      reads=[rawB, B("sperm")], writes=[bks[j]])
            c0 = half * 1024
            em.op("dve", lambda e, o_=rtmp[64:128, :],
                  i0=rawv[64:128, c0:c0 + 1024], i1=cosb[64:128, c0:c0 + 1024]:
                  e.tensor_tensor(o_, i0, i1, ALU.mult),
                  reads=[rawB, B("cosb")], writes=[B("rtmp")])
            em.op("dve", lambda e, o_=rtmp2[64:128, :],
                  i0=pst[64:128, cb:cb + 1024], i1=sinb[64:128, c0:c0 + 1024]:
                  e.tensor_tensor(o_, i0, i1, ALU.mult),
                  reads=[bks[0], bks[1], B("sinb")],
                  writes=[B("rtmp2")])
            em.op("dve", lambda e, o_=outv[64:128, c0:c0 + 1024],
                  i0=rtmp[64:128, :], i1=rtmp2[64:128, :]:
                  e.tensor_tensor(o_, i0, i1, ALU.add),
                  reads=[B("rtmp"), B("rtmp2")], writes=[outB])

    def ph_att_prologue(b):
        proB = B("prow")
        dma("q_pool", prow[:], wkvd_c.rearrange("p a t c -> p (a t c)"),
            writes=[proB])
        for rt in range(4):
            for sb_ in range(4):
                bk = pbank[sb_]
                for a in range(16):
                    mm(psA[:, sb_ * 512:(sb_ + 1) * 512],
                       prow[:, (a * 4 + rt) * 128:(a * 4 + rt) * 128 + 128],
                       xnv[:, a * S + sb_ * 512: a * S + (sb_ + 1) * 512],
                       start=(a == 0), stop=(a == 15),
                       reads=[proB, xnb[a]], wbank=[bk], grp=grp_of(a, 16))
                em.op("dve", lambda e,
                      o_=ckv[:, rt * S + sb_ * 512: rt * S + (sb_ + 1) * 512],
                      i_=psA[:, sb_ * 512:(sb_ + 1) * 512]:
                      e.tensor_copy(o_, i_), reads=[bk], writes=[ckvB[rt]],
                      grp=grp_of(sb_, 4))
            if "dbg_ckv" in dbg_out and b == 0:
                dma("q_act", dbg_out["dbg_ckv"][rt], ckv[:, rt * S:(rt + 1) * S],
                    reads=[ckvB[rt]])
        dma("q_pool", prow[:, 0:1024], wkrope_c.rearrange("p a c -> p (a c)"),
            writes=[proB])
        kb = [pbank[4 + i] for i in range(4)]
        for sb_ in range(4):
            for a in range(16):
                mm(psB[64:128, sb_ * 512:(sb_ + 1) * 512],
                   prow[:, a * 64:(a + 1) * 64],
                   xnv[:, a * S + sb_ * 512: a * S + (sb_ + 1) * 512],
                   start=(a == 0), stop=(a == 15),
                   reads=[proB, xnb[a]], wbank=[kb[sb_]], grp=grp_of(a, 16))
        em.op("act", lambda e, o_=qraw[64:128, :], i_=psB[64:128, 0:2048]:
              e.activation(o_, i_, AF.Copy), reads=kb, writes=[B("qraw")])
        if "dbg_kraw" in dbg_out and b == 0:
            dma("q_act", dbg_out["dbg_kraw"], qraw[:], reads=[B("qraw")])
        rope(qraw, B("qraw"), krope, B("krope"),
             lambda half: (psB, half * 1024, [kb[half * 2], kb[half * 2 + 1]]))
        if "dbg_krope" in dbg_out and b == 0:
            dma("q_act", dbg_out["dbg_krope"], krope[:], reads=[B("krope")])
        dma("q_pool", prow[:], wvup_c.rearrange("p a c -> p (a c)"), writes=[proB])
        for st in range(16):
            for ob in range(4):
                bk = pbank[ob]
                for a in range(4):
                    mm(psA[:, ob * 512:(ob + 1) * 512],
                       ckv[:, a * S + st * 128: a * S + st * 128 + 128],
                       prow[:, a * 2048 + ob * 512: a * 2048 + (ob + 1) * 512],
                       start=(a == 0), stop=(a == 3),
                       reads=[ckvB[a], proB], wbank=[bk], grp=grp_of(a, 4))
            em.op("dve", lambda e, o_=vall[:, st * 2048:(st + 1) * 2048],
                  i_=psA[:, 0:2048]:
                  e.tensor_copy(o_, i_),
                  reads=[pbank[0], pbank[1], pbank[2], pbank[3]],
                  writes=[vB[st]])
        em.op("dve", lambda e, o_=kstk[64:128, :], i_=krope[64:128, :]:
              e.tensor_copy(o_, i_), reads=[B("krope")],
              writes=[B("kstk_hi"), B("prow")])

    def ph_head(b, h):
        par = h % 2
        G1 = [pbank[i] for i in (range(4) if par == 0 else range(4, 8))]
        G2 = [pbank[i] for i in (range(4, 8) if par == 0 else range(4))]
        P1 = psA if par == 0 else psB
        P2 = psB if par == 0 else psA
        wqv = wqs_v[par]
        wqB = B(f"wqs{par}")
        wkv = wkups[:]
        wkB = B("wkups")
        wq_w = [wqB, B("prow")] if h < 2 else [wqB]
        dma("q_pool", wqv[:], wq_c[h].rearrange("p a c -> p (a c)"), writes=wq_w)
        dma("q_pool", wkv, wkup_c[h].rearrange("p a c -> p (a c)"), writes=[wkB])
        for sb_ in range(4):
            kb_ = G2[2 + sb_ % 2]
            col = 1024 + (sb_ % 2) * 512
            for a in range(4):
                mm(P2[0:64, col:col + 512],
                   wkv[:, a * 64:(a + 1) * 64],
                   ckv[:, a * S + sb_ * 512: a * S + (sb_ + 1) * 512],
                   start=(a == 0), stop=(a == 3),
                   reads=[wkB, ckvB[a]], wbank=[kb_], grp=grp_of(a, 4))
            em.op("act", lambda e, o_=kstk[0:64, sb_ * 512:(sb_ + 1) * 512],
                  i_=P2[0:64, col:col + 512]:
                  e.activation(o_, i_, AF.Copy),
                  reads=[kb_], writes=[B("kstk_lo")], grp=grp_of(sb_, 4))
        for sb_ in range(4):
            for a in range(16):
                mm(P2[:, sb_ * 512:(sb_ + 1) * 512],
                   wqv[:, a * 128:(a + 1) * 128],
                   xnv[:, a * S + sb_ * 512: a * S + (sb_ + 1) * 512],
                   start=(a == 0), stop=(a == 15),
                   reads=[wqB, xnb[a]], wbank=[G2[sb_]], grp=grp_of(a, 16))
            em.op("act", lambda e, o_=qstk[:, sb_ * 512:(sb_ + 1) * 512],
                  i_=P2[:, sb_ * 512:(sb_ + 1) * 512]:
                  e.activation(o_, i_, AF.Copy, scale=SCL),
                  reads=[G2[sb_]], writes=[B("qstk_lo"), B("qstk_hi")],
                  grp=grp_of(sb_, 4))
        rope(qstk, B("qstk_hi"), qstk, B("qstk_hi"),
             lambda half: (P1, 1024, [G1[2], G1[3]]))
        qkB = [B("kstk_lo"), B("kstk_hi"), B("qstk_lo"), B("qstk_hi")]
        ptB = [B("pts0"), B("pts1"), B("pts2")]
        OB_ = G1[2]
        RB_ = G1[3]
        slots = [(P1, 0, G1[0]), (P1, 512, G1[1]), (P2, 0, G2[0]), (P2, 512, G2[1])]
        for sb_ in range(4):
            T = 4 * sb_ + 4
            q0 = sb_ * 512

            def emit_pv(t):
                off = max(0, t - 4 * sb_) * 128
                pt = pts[:, (t % 3) * 512 + off: (t % 3) * 512 + 512]
                mm(P1[:, 2 * 512 + off: 3 * 512],
                   vall[:, t * 2048 + h * 128: t * 2048 + (h + 1) * 128],
                   pt, start=(t == 0), stop=(t == T - 1),
                   reads=[vB[t], ptB[t % 3]], wbank=[OB_], grp=grp_of(t, T))
                mm(P1[:, 3 * 512 + off: 4 * 512], ones_bf[:], pt,
                   start=(t == 0), stop=(t == T - 1),
                   reads=[B("ones_bf"), ptB[t % 3]], wbank=[RB_],
                   grp=grp_of(t, T))

            pend = []
            for t in range(T):
                qt_local = t - 4 * sb_
                off = max(0, qt_local) * 128
                pst, col, scb = slots[t % 4]
                scv = pst[:, col + off: col + 512]
                mm(scv, kstk[:, t * 128:(t + 1) * 128],
                   qstk[:, q0 + off: q0 + 512],
                   start=True, stop=True, reads=qkB, wbank=[scb])
                if qt_local >= 0:
                    em.op("dve", lambda e, o_=pst[:, col + off: col + off + 128],
                          i1=maskb[:]:
                          e.tensor_tensor(o_, o_, i1, ALU.add),
                          reads=[scb, B("maskb")], writes=[scb])
                pt = pts[:, (t % 3) * 512 + off: (t % 3) * 512 + 512]
                em.op("act", lambda e, o_=pt, i_=scv:
                      e.activation(o_, i_, AF.Exp),
                      reads=[scb], writes=[ptB[t % 3]])
                pend.append(t)
                if len(pend) > 2:
                    emit_pv(pend.pop(0))
            for t in pend:
                emit_pv(t)
            em.op("act", lambda e, o_=rln[:], i_=P1[:, 3 * 512:4 * 512]:
                  e.activation(o_, i_, AF.Ln), reads=[RB_], writes=[B("rln")])
            em.op("act", lambda e, o_=rrec[:], i_=rln[:]:
                  e.activation(o_, i_, AF.Exp, scale=-1.0),
                  reads=[B("rln")], writes=[B("rrec")])
            ov = obf[:, (sb_ % 2) * 512:(sb_ % 2) * 512 + 512]
            ovB = B(f"obf{sb_ % 2}")
            em.op("dve", lambda e, o_=ov, i0=P1[:, 2 * 512:3 * 512], i1=rrec[:]:
                  e.tensor_tensor(o_, i0, i1, ALU.mult),
                  reads=[OB_, B("rrec")], writes=[ovB])
            dma("q_pool",
                osp[b, 4 * sb_:4 * sb_ + 4, :, h, :].transpose([1, 0, 2]),
                ov[:].rearrange("p (j c) -> p j c", j=4),
                reads=[ovB],
                writes=[B(f"osp{b}_{4 * sb_ + j}") for j in range(4)])
            if "dbg_o" in dbg_out:
                for j in range(4):
                    qt = 4 * sb_ + j
                    dma("q_act", dbg_out["dbg_o"][b, qt, :, h, :],
                        ov[:, j * 128:(j + 1) * 128], reads=[ovB])

    def ph_ffn(b):
        hB = [B(f"hid{t}") for t in range(HT)]
        xnsB = B("xns")
        w1B = [B("w1s0"), B("w1s1")]
        w2B = [B("w2s0"), B("w2s1")]
        w3B = B("w3col")
        woB = B("wocol")
        ocB = [B("ocol0"), B("ocol1")]
        xrB = [B("xres0"), B("xres1")]
        outB = [B("outb0"), B("outb1")]
        w3deps = [B(f"w2q_{t}") for t in range(HT)]
        for sb2 in range(2):
            dma("q_sp", xns[:].rearrange("p (a c) -> p a c", a=16),
                xnsp[b, :, :, sb2 * 1024:(sb2 + 1) * 1024].transpose([1, 0, 2]),
                reads=[B(f"xnsp{b}_{a}") for a in range(16)],
                writes=[xnsB, w3B])
            w1fB = [B("w1f0"), B("w1f1")]
            w2fB = [B("w2f0"), B("w2f1")]

            def quant_w12(t):
                sl = t % 2
                dma("q_sp", w1f[sl][:],
                    w1_c[t].rearrange("p a c -> p (a c)"), writes=[w1fB[sl]])
                dma("q_sp", w2f[sl][:],
                    w2_c[t].rearrange("p a c -> p (a c)"), writes=[w2fB[sl]])
                for (wf, wfB, ws, wsB, rc) in (
                        (w1f[sl], w1fB[sl], w1s[sl], w1B[sl], small[:, 4:5]),
                        (w2f[sl], w2fB[sl], w2s[sl], w2B[sl], small[:, 5:6])):
                    em.op("dve", lambda e, o_=wf[:], s_=rc:
                          e.tensor_scalar(o_, o_, s_, MAGIC, ALU.mult, ALU.add),
                          reads=[wfB, B("qrec0"), B("qrec1")], writes=[wfB])
                    em.op("dve", lambda e, o_=wf[:]:
                          e.tensor_scalar(o_, o_, MAGIC, -1.0, ALU.subtract,
                                          ALU.max),
                          reads=[wfB], writes=[wfB])
                    em.op("dve", lambda e, o_=ws[:], i_=wf[:]:
                          e.tensor_scalar_min(o_, i_, 1.0),
                          reads=[wfB], writes=[wsB])

            quant_w12(0)
            for t in range(HT):
                if t + 1 < HT:
                    quant_w12(t + 1)
                if b == 0:
                    qastep(3)
                sl = t % 2
                b0 = 0 if sl == 0 else 4
                PX = psA if sl == 0 else psB
                for a in range(16):
                    for nb_ in range(2):
                        mm(PX[:, nb_ * 512:(nb_ + 1) * 512],
                           w1s[sl][:, a * 128:(a + 1) * 128],
                           xns[:, a * 1024 + nb_ * 512: a * 1024 + (nb_ + 1) * 512],
                           start=(a == 0), stop=(a == 15),
                           reads=[w1B[sl], xnsB], wbank=[pbank[b0 + nb_]],
                           grp=grp_of(a, 16))
                for a in range(16):
                    for nb_ in range(2):
                        mm(PX[:, 1024 + nb_ * 512: 1024 + (nb_ + 1) * 512],
                           w2s[sl][:, a * 128:(a + 1) * 128],
                           xns[:, a * 1024 + nb_ * 512: a * 1024 + (nb_ + 1) * 512],
                           start=(a == 0), stop=(a == 15),
                           reads=[w2B[sl], xnsB], wbank=[pbank[b0 + 2 + nb_]],
                           grp=grp_of(a, 16))
                em.op("act", lambda e, o_=sgb[:], i_=PX[:, 0:1024]:
                      e.activation(o_, i_, AF.Sigmoid),
                      reads=[pbank[b0], pbank[b0 + 1]], writes=[B("sgb")])
                em.op("dve", lambda e, o_=hid(t)[:], i0=PX[:, 1024:2048],
                      i1=sgb[:]:
                      e.tensor_tensor(o_, i0, i1, ALU.mult),
                      reads=[pbank[b0 + 2], pbank[b0 + 3], B("sgb")],
                      writes=[hB[t]])
                if "dbg_hid" in dbg_out and b == 0 and sb2 == 0:
                    dma("q_act", dbg_out["dbg_hid"][t], hid(t)[:], reads=[hB[t]])
            if b == 0:
                qadrain()
            for ob in range(4):
                w3BA = B("w3colA")
                dma("q_sp",
                    w3col[:, 32 * 512:HT * 512].rearrange(
                        "p (h c) -> p h c", h=HT - 32),
                    w3q[32:HT, :, ob, :].transpose([1, 0, 2]),
                    reads=w3deps, writes=[w3BA, B("w1f0"), B("w1f1")])
                dma("q_sp",
                    w3col[:, 0:32 * 512].rearrange("p (h c) -> p h c", h=32),
                    w3q[0:32, :, ob, :].transpose([1, 0, 2]),
                    reads=w3deps, writes=[w3B, xnsB])
                dma("q_pool", wocol[:].rearrange("p (a c) -> p a c", a=16),
                    wo_c[:, :, ob, :],
                    writes=[woB, B("w1f1"), B("w2f0"), B("w2f1")])
                for qt in range(8):
                    qtg = sb2 * 8 + qt
                    sl = qt % 2
                    dma("q_sp", ocol[sl][:],
                        osp[b, qtg].rearrange("p d c -> p (d c)"),
                        reads=[B(f"osp{b}_{qtg}")],
                        writes=[ocB[sl]] + ([B("w2f1")] if sl == 0 else []))
                    dma("q_act", xres[sl][:],
                        xr[b, qtg * 128:(qtg + 1) * 128, ob * 512:(ob + 1) * 512],
                        writes=[xrB[sl]])
                    ybank = pbank[qt % 4]
                    yv = ps(qt % 4)
                    for hc in range(HT):
                        mm(yv, hid(hc)[:, qt * 128:(qt + 1) * 128],
                           w3col[:, hc * 512:(hc + 1) * 512],
                           start=(hc == 0), stop=False,
                           reads=[hB[hc], w3B if hc < 32 else B("w3colA")],
                           wbank=[ybank],
                           grp=("start" if hc == 0 else "mid"))
                    for dc in range(16):
                        mm(yv, ocol[sl][:, dc * 128:(dc + 1) * 128],
                           wocol[:, dc * 512:(dc + 1) * 512],
                           start=False, stop=(dc == 15),
                           reads=[ocB[sl], woB], wbank=[ybank],
                           grp=("stop" if dc == 15 else "mid"))
                    em.op("dve", lambda e, o_=outb[sl][:], i0=yv, i1=xres[sl][:]:
                          e.tensor_tensor(o_, i0, i1, ALU.add),
                          reads=[ybank, xrB[sl]], writes=[outB[sl]])
                    dma("q_act",
                        y[b, qtg * 128:(qtg + 1) * 128, ob * 512:(ob + 1) * 512],
                        outb[sl][:], reads=[outB[sl]])

    for b in range(NB):
        em.fence()
        ph_ln(b)
        if stop_after == "ln":
            continue
        em.fence()
        ph_att_prologue(b)
        em.fence()
        for h in range(H):
            ph_head(b, h)
            qstep(8)
    if stop_after == "full":
        qdrain()
        if "dbg_w1q" in dbg_out:
            for t in range(HT):
                dma("q_act", dbg_out["dbg_w3q"][t], w3q[t],
                    reads=[B(f"w2q_{t}")])
            dma("q_act", dbg_out["dbg_gam"], small[:, 0:8], reads=[B("qrec0"), B("qrec1"), B("qrec2")])
        em.fence()
        for b in range(NB):
            ph_ffn(b)

    em.wait_all("sp", QUEUES)
    em.wait_all("act", ("q_act",))
    em.wait_all("pool", ("q_pool",))

    import sys
    print("EM op counts:", em.stats(), file=sys.stderr)

    em.finalize(nc)
    with nc.Block() as block:
        block.tensor(lambda e: em.emit_engine("pe", e))
        block.scalar(lambda e: em.emit_engine("act", e))
        block.vector(lambda e: em.emit_engine("dve", e))
        block.gpsimd(lambda e: em.emit_engine("pool", e))
        block.sync(lambda e: em.emit_engine("sp", e))
    return nc


def _make_consts(wq, wkvd, wkup, wkrope, wvup, wo, w1, w2, w3):
    f = np.float32
    wq_c = np.ascontiguousarray(
        wq.T.reshape(16, 128, 16, 128).transpose(2, 1, 0, 3))
    wkvd_c = np.ascontiguousarray(
        wkvd.T.reshape(16, 128, 4, 128).transpose(1, 0, 2, 3))
    wkup_c = np.ascontiguousarray(
        wkup.T.reshape(4, 128, 16, 64).transpose(2, 1, 0, 3))
    wkrope_c = np.ascontiguousarray(
        wkrope.T.reshape(16, 128, 64).transpose(1, 0, 2))
    wvup_c = np.ascontiguousarray(
        wvup.T.reshape(4, 128, 2048).transpose(1, 0, 2))
    wo_c = np.ascontiguousarray(
        wo.T.reshape(16, 128, 4, 512).transpose(1, 0, 2, 3))
    w1_c = np.ascontiguousarray(
        w1.T.reshape(16, 128, HT, 128).transpose(2, 1, 0, 3))
    w2_c = np.ascontiguousarray(
        w2.T.reshape(16, 128, HT, 128).transpose(2, 1, 0, 3))
    w3_c = np.ascontiguousarray(w3.T.reshape(HT, 128, 4, 512))
    ones = np.ones((128, 128), f)
    Sm = np.zeros((64, 64), f)
    for i in range(32):
        Sm[2 * i, 2 * i + 1] = -1.0
        Sm[2 * i + 1, 2 * i] = 1.0
    sperm = np.ascontiguousarray(Sm.T)
    pos = np.arange(S, dtype=np.float64)
    freqs = 1.0 / (10000.0 ** (np.arange(0, RD, 2, dtype=np.float64) / RD))
    ang = pos[:, None] * freqs[None, :]
    cos = np.ascontiguousarray(np.repeat(np.cos(ang), 2, axis=1).T).astype(f)
    sin = np.ascontiguousarray(np.repeat(np.sin(ang), 2, axis=1).T).astype(f)
    kk = np.arange(128)[:, None]
    qq = np.arange(128)[None, :]
    mask = np.where(kk > qq, -1e30, 0.0).astype(f)
    return {
        "wq_c": wq_c, "wkvd_c": wkvd_c, "wkup_c": wkup_c,
        "wkrope_c": wkrope_c, "wvup_c": wvup_c, "wo_c": wo_c,
        "w1_c": w1_c, "w2_c": w2_c, "w3_c": w3_c, "ones": ones,
        "sperm": sperm, "cos": cos, "sin": sin, "mask": mask,
        "scal": np.tile(np.array([[LNEPS, MAGIC, -MAGIC]], f), (128, 1)),
    }


def _in_map(core, x, consts):
    xb = x[2 * core:2 * core + 2]
    m = {
        "xT": np.ascontiguousarray(xb.transpose(0, 2, 1)).reshape(NB, 16, 128, S),
        "xr": np.ascontiguousarray(xb),
        "c_ones": consts["ones"], "c_sperm": consts["sperm"],
        "c_cos": consts["cos"], "c_sin": consts["sin"],
        "c_mask": consts["mask"],
        "c_scal": consts["scal"],
    }
    for k in ("wq_c", "wkvd_c", "wkup_c", "wkrope_c", "wvup_c", "wo_c",
              "w1_c", "w2_c", "w3_c"):
        m[k] = consts[k]
    return m


def kernel(L0_attn_x, L0_attn_wq, L0_attn_w_kv_down, L0_attn_w_k_up,
           L0_attn_w_k_rope, L0_attn_w_v_up, L0_attn_wo, L0_ffn_w1,
           L0_ffn_w2, L0_ffn_w3):
    x = np.asarray(L0_attn_x, np.float32)
    consts = _make_consts(
        np.asarray(L0_attn_wq, np.float32),
        np.asarray(L0_attn_w_kv_down, np.float32),
        np.asarray(L0_attn_w_k_up, np.float32),
        np.asarray(L0_attn_w_k_rope, np.float32),
        np.asarray(L0_attn_w_v_up, np.float32),
        np.asarray(L0_attn_wo, np.float32),
        np.asarray(L0_ffn_w1, np.float32),
        np.asarray(L0_ffn_w2, np.float32),
        np.asarray(L0_ffn_w3, np.float32))
    nc = build_nc()
    in_maps = [_in_map(c, x, consts) for c in range(8)]
    res = run_bass_kernel_spmd(nc, in_maps, core_ids=list(range(8)))
    out = np.concatenate([np.asarray(r["y"]) for r in res.results], axis=0)
    return out.astype(np.float32)
